# version 09
# baseline (speedup 1.0000x reference)
import os
import numpy as np
import ml_dtypes
from contextlib import ExitStack

import concourse.bass as bass
import concourse.mybir as mybir
from concourse.bass_utils import run_bass_kernel_spmd
from concourse.alu_op_type import AluOpType as Alu

F32 = mybir.dt.float32
BF16 = mybir.dt.bfloat16
U8 = mybir.dt.uint8
AF = mybir.ActivationFunctionType
BFD = ml_dtypes.bfloat16

NCORES = 8
D = 1024
DH = 512
T = 2048
NSEQ = 4
L = 128
NG = T // L
MCN = 8
DCN = 4
NSLOT = 4
EPS_LN = 1e-5
EPS_RMS = 1e-6
SCALE_Q = float(np.float32(1.0) / np.sqrt(np.float32(DH)))
NEGBIG = -1e30

STAGE = 3




class Asm:
    ENGS = ("pe", "dve", "act", "sp", "pool")
    NLANE = 6

    def __init__(self):
        self.ops = []

    def op(self, eng, fn, r=(), w=(), dma=False, fragile=False):
        o = {"e": eng, "fn": fn, "r": tuple(r), "w": tuple(w), "dma": dma,
             "i": len(self.ops), "inc": bool(dma), "needs": {}, "lane": None,
             "fragile": fragile}
        self.ops.append(o)
        return o

    def sem_key(self, o):
        if o["dma"]:
            return f"{o['e']}_dma{o['lane']}"
        return o["e"]

    def resolve(self):
        lane_ctr = {}
        for o in self.ops:
            if o["dma"]:
                c = lane_ctr.get(o["e"], 0)
                o["lane"] = c % self.NLANE
                lane_ctr[o["e"]] = c + 1
        lastw = {}
        readers = {}
        for o in self.ops:
            needs = {}

            def need(t):
                if (not t["dma"]) and t["e"] == o["e"] and not o["fragile"]:
                    return
                k = self.sem_key(t)
                prev = needs.get(k)
                if prev is None or t["i"] > prev["i"]:
                    needs[k] = t

            for k in o["r"]:
                t = lastw.get(k)
                if t is not None:
                    need(t)
            for k in o["w"]:
                t = lastw.get(k)
                if t is not None:
                    need(t)
                for t2 in readers.get(k, ()):
                    need(t2)
            o["needs"] = needs
            for k in o["r"]:
                readers.setdefault(k, []).append(o)
            for k in o["w"]:
                lastw[k] = o
                readers[k] = []
        for o in self.ops:
            for t in o["needs"].values():
                t["inc"] = True
        cum = {}
        prev_lane_cum = {}
        for o in self.ops:
            sk = self.sem_key(o)
            if o["dma"]:
                o["gate"] = cum.get(sk, 0)
            if o["inc"]:
                cum[sk] = cum.get(sk, 0) + (16 if o["dma"] else 1)
            o["cum"] = cum.get(sk, 0)
        self.final = dict(cum)
        maxw = {e: {} for e in self.ENGS}
        for o in self.ops:
            o["waits"] = []
            gates = []
            if o["dma"] and o.get("gate", 0) > 0:
                gates.append((self.sem_key(o), o["gate"]))
            for sk, v in gates + sorted(
                    [(sk, t["cum"]) for sk, t in o["needs"].items()]):
                if v > maxw[o["e"]].get(sk, 0):
                    o["waits"].append((sk, v))
                    maxw[o["e"]][sk] = v

    hop = {}

    def play(self, eng_name, eng, sems):
        import bass_rust as _br
        DI = _br.DependencyInfo(_br.DependencyClass.Sync,
                                _br.DependencyReason.DATA)
        last = None

        def chain(bi):
            nonlocal last
            try:
                ins = bi.ins
            except Exception:
                return
            if last is not None:
                try:
                    ins.add_dependency(last, DI)
                except Exception:
                    pass
            last = ins.name

        for o in self.ops:
            if o["e"] != eng_name:
                continue
            for sk, v in o["waits"]:
                w = eng.wait_ge(sems[sk], v)
                if w is not None:
                    chain(w)
            inst = o["fn"](eng)
            chain(inst)
            if o["inc"]:
                h = None if o["dma"] else self.hop.get(o["e"])
                if h is not None:
                    h2 = h(eng)
                    chain(h2)
                    h2.then_inc(sems[o["e"]], 1)
                else:
                    inst.then_inc(sems[self.sem_key(o)], 16 if o["dma"] else 1)


def build_program(stage=STAGE, dbg=False):
    nc = bass.Bass("TRN2", target_bir_lowering=False)
    A = Asm()

    dx = nc.dram_tensor("x", [NSEQ, T, D], F32, kind="ExternalInput").ap()
    dout = nc.dram_tensor("out", [NSEQ, T, D], F32, kind="ExternalOutput").ap()
    dwqT = nc.dram_tensor("wqT", [D, DH], F32, kind="ExternalInput").ap()
    dwkT = nc.dram_tensor("wkT", [D, DH], F32, kind="ExternalInput").ap()
    dwvT = nc.dram_tensor("wvT", [D, DH], F32, kind="ExternalInput").ap()
    dwoT = nc.dram_tensor("woT", [D, DH], F32, kind="ExternalInput").ap()
    dwoutT = nc.dram_tensor("woutT", [DH, D], F32, kind="ExternalInput").ap()
    dnw = nc.dram_tensor("nwr", [128, 8], F32, kind="ExternalInput").ap()
    dwi = nc.dram_tensor("wir", [128, 8], F32, kind="ExternalInput").ap()
    dwf = nc.dram_tensor("wfr", [128, 8], F32, kind="ExternalInput").ap()
    dbo = nc.dram_tensor("bor", [1, DH], F32, kind="ExternalInput").ap()
    dsc = nc.dram_tensor("scal", [1, 8], F32, kind="ExternalInput").ap()
    dk0 = nc.dram_tensor("k0u", [2 * D, 4096], U8, kind="ExternalInput").ap()
    dk1 = nc.dram_tensor("k1u", [4096, 4096], U8, kind="ExternalInput").ap()
    dk2 = nc.dram_tensor("k2u", [4096, D], U8, kind="ExternalInput").ap()
    didb = nc.dram_tensor("idb", [128, 128], BF16, kind="ExternalInput").ap()
    didf = nc.dram_tensor("idf", [128, 128], F32, kind="ExternalInput").ap()
    dmask = nc.dram_tensor("mskn", [128, 128], F32, kind="ExternalInput").ap()
    if dbg:
        ddbg_rb = nc.dram_tensor("dbg_rb", [1, 3072], F32, kind="ExternalOutput").ap()
        ddbg_cols = nc.dram_tensor("dbg_cols", [128, 24], F32, kind="ExternalOutput").ap()
        ddbg_ksb = nc.dram_tensor("dbg_ksb", [128, 2048], F32, kind="ExternalOutput").ap()
        ddbg_qsb = nc.dram_tensor("dbg_qsb", [128, 2048], F32, kind="ExternalOutput").ap()
        ddbg_htok = nc.dram_tensor("dbg_htok", [128, 512], F32, kind="ExternalOutput").ap()
        ddbg_vtok = nc.dram_tensor("dbg_vtok", [128, 512], F32, kind="ExternalOutput").ap()
        ddbg_otok = nc.dram_tensor("dbg_otok", [128, 512], F32, kind="ExternalOutput").ap()
        ddbg_ct = nc.dram_tensor("dbg_ct", [128, 8192], F32, kind="ExternalOutput").ap()
        ddbg_nst = nc.dram_tensor("dbg_nst", [128, 16], F32, kind="ExternalOutput").ap()
        ddbg_nrm = nc.dram_tensor("dbg_nrm", [128, 4096], F32, kind="ExternalOutput").ap()
        ddbg_sbf = nc.dram_tensor("dbg_sbf", [128, 1024], F32, kind="ExternalOutput").ap()
        ddbg_k2 = nc.dram_tensor("dbg_k2", [128, 2048], F32, kind="ExternalOutput").ap()
        ddbg_kt = nc.dram_tensor("dbg_kt", [128, 512], F32, kind="ExternalOutput").ap()
        ddbg_s3 = nc.dram_tensor("dbg_s3", [128, 128], F32, kind="ExternalOutput").ap()
        ddbg_d3 = nc.dram_tensor("dbg_d3", [128, 128], F32, kind="ExternalOutput").ap()
        ddbg_st = nc.dram_tensor("dbg_st", [128, 4096], F32, kind="ExternalOutput").ap()
    dw0 = nc.dram_tensor("w0s", [8, 128, 4096], BF16, kind="Internal").ap()
    dw1 = nc.dram_tensor("w1s", [32, 128, 4096], BF16, kind="Internal").ap()
    dw2 = nc.dram_tensor("w2s", [32, 128, 1024], BF16, kind="Internal").ap()
    dwout = nc.dram_tensor("wouts", [128, 4, 1024], BF16, kind="Internal").ap()

    es = ExitStack()
    sb = lambda n, s, d: es.enter_context(nc.sbuf_tensor(n, s, d))
    ps = lambda n, s, d: es.enter_context(nc.psum_tensor(n, s, d))

    x_sb = sb("x_sb", [128, 4096], F32)
    xin = sb("xin", [128, 1024], F32)
    xout = sb("xout", [128, 1024], F32)
    nrm = sb("nrm", [128, 4096], BF16)
    h0 = sb("h0", [128, 16384], BF16)
    h1r = sb("h1r", [128, 4096], BF16)
    qsb = sb("qsb", [128, 2048], F32)
    ksb = sb("ksb", [128, 2048], BF16)
    sbf = sb("sbf", [128, 1024], BF16)
    ktok = sb("ktok", [128, 512], BF16)
    vtok = sb("vtok", [128, 512], BF16)
    otok = sb("otok", [128, 512], BF16)
    htok = sb("htok", [128, 512], F32)
    ztmp = sb("ztmp", [128, 512], F32)
    CT = sb("CT", [128, 8192], F32)
    nst = sb("nst", [128, 16], F32)
    wring = sb("wring", [128, NSLOT * 4096], BF16)
    rb = sb("rb", [1, 3072], F32)
    rbb = sb("rbb", [1, 1024], BF16)
    cols = sb("cols", [128, 24], F32)
    params = sb("params", [128, 8], F32)
    nwsb = sb("nwsb", [128, 8], F32)
    wisb = sb("wisb", [128, 8], F32)
    wfsb = sb("wfsb", [128, 8], F32)
    wib_b = sb("wib_b", [128, 8], BF16)
    wfb_b = sb("wfb_b", [128, 8], BF16)
    wq = sb("wq", [128, 4096], BF16)
    wk = sb("wk", [128, 4096], BF16)
    wv = sb("wv", [128, 4096], BF16)
    wo = sb("wo", [128, 4096], BF16)
    bo_bc = sb("bo_bc", [128, 512], BF16)
    identf = sb("identf", [128, 128], F32)
    maskn = sb("maskn", [128, 128], F32)
    onesr = sb("onesr", [1, 128], F32)
    onescb = sb("onescb", [128, 1], BF16)

    p = [ps(f"p{i}", [128, 512], F32) for i in range(7)]
    psync = ps("psync", [128, 512], F32)
    PK = [f"P{i}" for i in range(7)] + ["PT"]

    HOME = {"MEAN": 0, "EX2": 1, "VAR": 2, "STD": 3, "RSTD": 4, "MR": 5,
            "ZI": 0, "ZF": 1, "MSC": 2, "FF": 3, "WW": 4, "UU": 5,
            "ALPHA": 2, "BETA": 3}

    def row(nm, c0=0, c1=512):
        j = HOME[nm]
        return rb[0:1, j * 512 + c0:j * 512 + c1]

    def rkey(nm):
        return f"R{HOME[nm]}"

    CU, CA, CB, CL, CD, CD2, CR = 0, 1, 2, 3, 4, 5, 6

    dum = cols
    import os as _os
    _hops = _os.environ.get("KHOP", "")
    if "p" in _hops:
        A.hop["pe"] = lambda e: e.matmul(psync[0:1, 0:1], onescb[:, 0:1],
                                         onescb[:, 0:1], start=True, stop=True)
    if "d" in _hops:
        A.hop["dve"] = lambda e: e.tensor_scalar(dum[0:1, 20:21], dum[0:1, 21:22],
                                                 1.0, None, Alu.mult)
    if "a" in _hops:
        A.hop["act"] = lambda e: e.activation(dum[0:1, 20:21], dum[0:1, 21:22], AF.Copy)

    def mm(out_ap, lhsT, rhs, r, w, start=True, stop=True):
        return A.op("pe", lambda e, o=out_ap, l=lhsT, rr=rhs, st=start, sp_=stop:
                    e.matmul(o, l, rr, start=st, stop=sp_), r=r, w=w)

    def trans(out_ap, in_ap, ident, r, w):
        return A.op("pe", lambda e, o=out_ap, i=in_ap, d=ident:
                    e.transpose(o, i, d), r=r, w=w)

    def dve(fn, r, w, fragile=False):
        return A.op("dve", fn, r=r, w=w, fragile=fragile)

    def act(fn, r, w):
        return A.op("act", fn, r=r, w=w)

    def dma(eng, out_ap, in_ap, r, w):
        return A.op(eng, lambda e, o=out_ap, i=in_ap: e.dma_start(out=o, in_=i),
                    r=r, w=w, dma=True)

    def bcast(row_ap, out_psum, rkeys, pkey, n=None):
        return mm(out_psum, onesr[:, 0:128], row_ap,
                  r=list(rkeys) + ["onesr"], w=[pkey])

    def col_of(row_ap, out_psum, rkeys, pkey):
        return mm(out_psum, row_ap, onesr[:, 0:1],
                  r=list(rkeys) + ["onesr"], w=[pkey])

    dve(lambda e: e.memset(onesr[:], 1.0), r=[], w=["onesr"])
    dve(lambda e: e.memset(onescb[:], 1.0), r=[], w=["onescb"])
    dve(lambda e: e.memset(CT[:], 0.0), r=[], w=["CT"])
    dve(lambda e: e.memset(nst[:], 0.0), r=[], w=["nst"])
    dve(lambda e: e.memset(cols[0:1, 16:20], NEGBIG), r=[], w=["mprev"])
    dve(lambda e: e.memset(cols[0:1, 21:22], EPS_LN), r=[], w=["epsln"])
    dve(lambda e: e.memset(cols[0:1, 22:23], EPS_RMS), r=[], w=["epsrm"])
    dma("sp", identf[:], didf, r=[], w=["identf"])
    dma("sp", maskn[:], dmask, r=[], w=["maskn"])
    dma("sp", nwsb[:], dnw, r=[], w=["nwsb"])
    dma("sp", wisb[:], dwi, r=[], w=["wisb"])
    dma("sp", wfsb[:], dwf, r=[], w=["wfsb"])
    dma("sp", cols[0:1, 8:16], dsc, r=[], w=["scal"])
    dma("sp", row("MR"), dbo, r=[], w=[rkey("MR")])
    dve(lambda e: e.tensor_tensor(wisb[:], wisb[:], nwsb[:], Alu.mult),
        r=["wisb", "nwsb"], w=["wisb"])
    dve(lambda e: e.tensor_tensor(wfsb[:], wfsb[:], nwsb[:], Alu.mult),
        r=["wfsb", "nwsb"], w=["wfsb"])
    dve(lambda e: e.tensor_copy(wib_b[:], wisb[:]), r=["wisb"], w=["wib_b"])
    dve(lambda e: e.tensor_copy(wfb_b[:], wfsb[:]), r=["wfsb"], w=["wfb_b"])
    for j, mult in ((0, 4.0), (1, 2.0), (2, 0.5)):
        dve(lambda e, j=j, m=mult: e.tensor_scalar(
            cols[0:1, 20:21], cols[0:1, 8 + j:9 + j], m, None, Alu.mult),
            r=["scal"], w=["c20"])
        bcast(cols[0:1, 20:21], p[6][:, 0:1], ["c20"], "P6")
        dve(lambda e, j=j: e.tensor_copy(params[:, j:j + 1], p[6][:, 0:1]),
            r=["P6"], w=["params"])
    bcast(row("MR"), p[5][:], [rkey("MR")], "P5")
    dve(lambda e: e.tensor_copy(bo_bc[:], p[5][:]), r=["P5"], w=["bo_bc"])

    for wsb, ddr, scl in ((wq, dwqT, SCALE_Q), (wk, dwkT, SCALE_Q),
                          (wv, dwvT, 1.0), (wo, dwoT, 1.0)):
        nm = wsb.name
        for mc in range(MCN):
            dma("sp", ztmp[:, 0:512], ddr[mc * 128:(mc + 1) * 128, :],
                r=[], w=["ztmp"])
            dve(lambda e, wsb=wsb, mc=mc, scl=scl: e.tensor_scalar(
                wsb[:, mc * 512:(mc + 1) * 512], ztmp[:, 0:512],
                nwsb[:, mc:mc + 1], scl, Alu.mult, Alu.mult),
                r=["ztmp", "nwsb"], w=[nm])
    for dc in range(DCN):
        for hh in range(2):
            dma("sp", ztmp[:, 0:512],
                dwoutT[dc * 128:(dc + 1) * 128, hh * 512:(hh + 1) * 512],
                r=[], w=["ztmp"])
            act(lambda e: e.activation(htok[:, 0:512], ztmp[:, 0:512], AF.Copy),
                r=["ztmp"], w=["htok"])
            dma("pool", dwout[:, dc, hh * 512:(hh + 1) * 512], htok[:, 0:512],
                r=["htok"], w=["dwout"])

    ha = lambda: h0[:, 0:4096]
    hb_ = lambda: h0[:, 4096:8192]
    hw = lambda: h0[:, 8192:12288]
    for kc in range(8):
        dma("pool", ha(), dk0[kc * 128:(kc + 1) * 128, :], r=["h0"], w=["h0"])
        dma("pool", hb_(), dk0[D + kc * 128:D + (kc + 1) * 128, :], r=["h0"], w=["h0"])
        dve(lambda e: e.tensor_tensor(hw(), ha(), hb_(), Alu.subtract),
            r=["h0"], w=["h0"])
        dma("pool", dw0[kc], hw(), r=["h0"], w=["w0s"])
    for kc in range(32):
        dma("pool", ha(), dk1[kc * 128:(kc + 1) * 128, :], r=["h0"], w=["h0"])
        dve(lambda e: e.tensor_scalar(hw(), ha(), 2.0, -1.0, Alu.mult, Alu.add),
            r=["h0"], w=["h0"])
        dma("pool", dw1[kc], hw(), r=["h0"], w=[f"w1s{kc // 8}"])
    for kc in range(32):
        dma("pool", ha()[:, 0:1024], dk2[kc * 128:(kc + 1) * 128, :], r=["h0"], w=["h0"])
        dve(lambda e: e.tensor_scalar(hw()[:, 0:1024], ha()[:, 0:1024],
                                      2.0, -1.0, Alu.mult, Alu.add),
            r=["h0"], w=["h0"])
        dma("pool", dw2[kc], hw()[:, 0:1024], r=["h0"], w=[f"w2s{kc // 8}"])

    ring_ctr = [0]

    def ring_fetch(src_ap, src_key, f3=None):
        s = ring_ctr[0] % NSLOT
        qq = "sp" if (ring_ctr[0] % 2 == 0) else "pool"
        ring_ctr[0] += 1
        slot = wring[:, s * 4096:(s + 1) * 4096]
        dst = slot.rearrange("p (k f) -> p k f", f=f3) if f3 else slot
        dma(qq, dst, src_ap, r=[src_key], w=[f"ring{s}"])
        return slot, f"ring{s}"

    def ln_stats():
        for mc in range(MCN):
            act(lambda e, mc=mc: e.activation(
                ha()[:, mc * 512:(mc + 1) * 512], x_sb[:, mc * 512:(mc + 1) * 512],
                AF.Copy), r=["x_sb"], w=["h0"])
            act(lambda e, mc=mc: e.activation(
                hb_()[:, mc * 512:(mc + 1) * 512], x_sb[:, mc * 512:(mc + 1) * 512],
                AF.Square), r=["x_sb"], w=["h0"])
        for mc in range(MCN):
            mm(p[5][0:1, :], onescb[:], ha()[:, mc * 512:(mc + 1) * 512],
               r=["onescb", "h0"], w=["P5"], start=(mc == 0), stop=(mc == MCN - 1))
        for mc in range(MCN):
            mm(p[6][0:1, :], onescb[:], hb_()[:, mc * 512:(mc + 1) * 512],
               r=["onescb", "h0"], w=["P6"], start=(mc == 0), stop=(mc == MCN - 1))
        dve(lambda e: e.tensor_scalar(row("MEAN"), p[5][0:1, :], 1.0 / D, None, Alu.mult),
            r=["P5"], w=[rkey("MEAN")])
        dve(lambda e: e.tensor_scalar(row("EX2"), p[6][0:1, :], 1.0 / D, None, Alu.mult),
            r=["P6"], w=[rkey("EX2")])
        dve(lambda e: e.tensor_tensor(row("STD"), row("MEAN"), row("MEAN"), Alu.mult),
            r=[rkey("MEAN")], w=[rkey("STD")])
        dve(lambda e: e.tensor_tensor(row("VAR"), row("EX2"), row("STD"), Alu.subtract),
            r=[rkey("EX2"), rkey("STD")], w=[rkey("VAR")])
        act(lambda e: e.activation(row("STD"), row("VAR"), AF.Sqrt,
                                   bias=cols[0:1, 21:22]),
            r=[rkey("VAR"), "epsln"], w=[rkey("STD")])
        dve(lambda e: e.reciprocal(row("RSTD"), row("STD")),
            r=[rkey("STD")], w=[rkey("RSTD")], fragile=True)

    sweep_no = [0]

    def ffn_sweep():
        first = (sweep_no[0] == 0)
        sweep_no[0] += 1
        for sbi in range(8):
            piece, pk = ring_fetch(
                dw0[:, :, sbi * 512:(sbi + 1) * 512].rearrange("k p f -> p k f"),
                "w0s", f3=512)
            for kc in range(8):
                for ocl in range(4):
                    mm(p[ocl][:], piece[:, kc * 512 + ocl * 128:kc * 512 + (ocl + 1) * 128],
                       nrm[:, kc * 512:(kc + 1) * 512],
                       r=[pk, f"nrm{kc}"], w=[PK[ocl]], start=(kc == 0), stop=(kc == 7))
            for ocl in range(4):
                oc = sbi * 4 + ocl
                act(lambda e, ocl=ocl, oc=oc: e.activation(
                    h0[:, oc * 512:(oc + 1) * 512], p[ocl][:], AF.Relu,
                    scale=params[:, 0:1]),
                    r=[PK[ocl], "params"], w=["h0"])
        for sbi in range(8):
            for pc in range(4):
                piece, pk = ring_fetch(
                    dw1[pc * 8:(pc + 1) * 8, :, sbi * 512:(sbi + 1) * 512]
                    .rearrange("k p f -> p k f"), f"w1s{pc}", f3=512)
                for kcl in range(8):
                    kc = pc * 8 + kcl
                    for ocl in range(4):
                        mm(p[ocl][:],
                           piece[:, kcl * 512 + ocl * 128:kcl * 512 + (ocl + 1) * 128],
                           h0[:, kc * 512:(kc + 1) * 512],
                           r=[pk, "h0"], w=[PK[ocl]],
                           start=(kc == 0), stop=(kc == 31))
            for ocl in range(4):
                oc = sbi * 4 + ocl
                slot = oc % 8
                act(lambda e, ocl=ocl, slot=slot: e.activation(
                    h1r[:, slot * 512:(slot + 1) * 512], p[ocl][:], AF.Relu,
                    scale=params[:, 1:2]),
                    r=[PK[ocl], "params"], w=[f"h1_{slot}"])
            if sbi % 2 == 1:
                rnd = sbi // 2
                for half in range(2):
                    piece, pk = ring_fetch(
                        dw2[rnd * 8:(rnd + 1) * 8, :, half * 512:(half + 1) * 512]
                        .rearrange("k p f -> p k f"), f"w2s{rnd}", f3=512)
                    for mcl in range(4):
                        mc = half * 4 + mcl
                        pp = 4 + (mc % 2)
                        for kcr in range(8):
                            mm(p[pp][:],
                               piece[:, kcr * 512 + mcl * 128:kcr * 512 + (mcl + 1) * 128],
                               h1r[:, kcr * 512:(kcr + 1) * 512],
                               r=[pk, f"h1_{kcr}"], w=[PK[pp]],
                               start=(kcr == 0), stop=(kcr == 7))
                        dve(lambda e, mc=mc, pp=pp: e.scalar_tensor_tensor(
                            x_sb[:, mc * 512:(mc + 1) * 512], p[pp][:],
                            params[:, 2:3], x_sb[:, mc * 512:(mc + 1) * 512],
                            Alu.mult, Alu.add),
                            r=[PK[pp], "params", "x_sb"], w=["x_sb"])

    def load_x_group(g):
        for q in range(NSEQ):
            dma("act", xin[:], dx[q, g * L:(g + 1) * L, :], r=[], w=["xin"])
            for mc in range(MCN):
                pp = mc % 2
                trans(p[pp][:, 0:128], xin[:, mc * 128:(mc + 1) * 128], identf[:],
                      r=["xin", "identf"], w=[PK[pp]])
                dve(lambda e, mc=mc, q=q, pp=pp: e.tensor_copy(
                    x_sb[:, mc * 512 + q * 128:mc * 512 + (q + 1) * 128],
                    p[pp][:, 0:128]),
                    r=[PK[pp]], w=["x_sb"])

    def store_x_group(g):
        for q in range(NSEQ):
            for mc in range(MCN):
                pp = mc % 2
                trans(p[pp][:, 0:128],
                      x_sb[:, mc * 512 + q * 128:mc * 512 + (q + 1) * 128], identf[:],
                      r=["x_sb", "identf"], w=[PK[pp]])
                dve(lambda e, mc=mc, pp=pp: e.tensor_copy(
                    xout[:, mc * 128:(mc + 1) * 128], p[pp][:, 0:128]),
                    r=[PK[pp]], w=["xout"])
            dma("act", dout[q, g * L:(g + 1) * L, :], xout[:], r=["xout"], w=["dout"])

    def ln_to_nrm():
        ln_stats()
        bcast(row("MEAN"), p[5][:], [rkey("MEAN")], "P5")
        bcast(row("RSTD"), p[6][:], [rkey("RSTD")], "P6")
        for mc in range(MCN):
            dve(lambda e, mc=mc: e.tensor_tensor(
                ztmp[:], x_sb[:, mc * 512:(mc + 1) * 512], p[5][:], Alu.subtract),
                r=["x_sb", "P5", "ztmp"], w=["ztmp"])
            dve(lambda e, mc=mc: e.tensor_tensor(
                nrm[:, mc * 512:(mc + 1) * 512], ztmp[:], p[6][:], Alu.mult),
                r=["ztmp", "P6"], w=[f"nrm{mc}"])

    def ln2_attn_in():
        ln_stats()
        dve(lambda e: e.tensor_scalar(row("EX2"), row("VAR"), EPS_LN, None, Alu.add),
            r=[rkey("VAR")], w=[rkey("EX2")])
        dve(lambda e: e.reciprocal(row("STD"), row("EX2")),
            r=[rkey("EX2")], w=[rkey("STD")], fragile=True)
        dve(lambda e: e.tensor_tensor(row("EX2"), row("VAR"), row("STD"), Alu.mult),
            r=[rkey("VAR"), rkey("STD")], w=[rkey("EX2")])
        act(lambda e: e.activation(row("STD"), row("EX2"), AF.Sqrt,
                                   bias=cols[0:1, 22:23]),
            r=[rkey("EX2"), "epsrm"], w=[rkey("STD")])
        dve(lambda e: e.reciprocal(row("UU"), row("STD")),
            r=[rkey("STD")], w=[rkey("UU")], fragile=True)
        bcast(row("MEAN"), p[6][:], [rkey("MEAN")], "P6")
        bcast(row("RSTD"), p[4][:], [rkey("RSTD")], "P4")
        bcast(row("UU"), p[5][:], [rkey("UU")], "P5")
        for mc in range(MCN):
            dve(lambda e, mc=mc: e.tensor_tensor(
                ztmp[:], x_sb[:, mc * 512:(mc + 1) * 512], p[6][:], Alu.subtract),
                r=["x_sb", "P6"], w=["ztmp"])
            dve(lambda e: e.tensor_tensor(ztmp[:], ztmp[:], p[4][:], Alu.mult),
                r=["ztmp", "P4"], w=["ztmp"])
            dve(lambda e, mc=mc: e.tensor_tensor(
                x_sb[:, mc * 512:(mc + 1) * 512], x_sb[:, mc * 512:(mc + 1) * 512],
                ztmp[:], Alu.add),
                r=["x_sb", "ztmp"], w=["x_sb"])
            dve(lambda e, mc=mc: e.tensor_tensor(
                nrm[:, mc * 512:(mc + 1) * 512], ztmp[:], p[5][:], Alu.mult),
                r=["ztmp", "P5"], w=[f"nrm{mc}"])

    def attn_group(g, dbg_this=False):
        for wsb, wkname, dest, destk in ((wq, "wq", qsb, "qsb"), (wk, "wk", ksb, "ksb")):
            for dc in range(DCN):
                pp = dc % 2
                for mc in range(MCN):
                    mm(p[pp][:],
                       wsb[:, mc * 512 + dc * 128:mc * 512 + (dc + 1) * 128],
                       nrm[:, mc * 512:(mc + 1) * 512],
                       r=[wkname, f"nrm{mc}"], w=[PK[pp]],
                       start=(mc == 0), stop=(mc == MCN - 1))
                act(lambda e, dest=dest, dc=dc, pp=pp: e.activation(
                    dest[:, dc * 512:(dc + 1) * 512], p[pp][:], AF.Copy),
                    r=[PK[pp]], w=[destk])
        if dbg_this:
            for c0 in range(0, 2048, 512):
                dve(lambda e, c0=c0: e.tensor_copy(ztmp[:, 0:512], ksb[:, c0:c0 + 512]),
                    r=["ksb"], w=["ztmp"])
                dma("act", ddbg_k2[:, c0:c0 + 512], ztmp[:], r=["ztmp"], w=["dbgout"])
        for mc in range(MCN):
            mm(p[2][0:1, :], wib_b[:, mc:mc + 1], nrm[:, mc * 512:(mc + 1) * 512],
               r=["wib_b", f"nrm{mc}"], w=["P2"], start=(mc == 0), stop=(mc == MCN - 1))
        for mc in range(MCN):
            mm(p[3][0:1, :], wfb_b[:, mc:mc + 1], nrm[:, mc * 512:(mc + 1) * 512],
               r=["wfb_b", f"nrm{mc}"], w=["P3"], start=(mc == 0), stop=(mc == MCN - 1))
        dve(lambda e: e.tensor_scalar(row("ZI"), p[2][0:1, :], cols[0:1, 11:12],
                                      None, Alu.add),
            r=["P2", "scal"], w=[rkey("ZI")])
        dve(lambda e: e.tensor_scalar(row("ZF"), p[3][0:1, :], cols[0:1, 12:13],
                                      None, Alu.add),
            r=["P3", "scal"], w=[rkey("ZF")])
        for q in range(NSEQ):
            qs = (q * 128, (q + 1) * 128)
            dve(lambda e, q=q, qs=qs: e.tensor_tensor_scan(
                row("MSC", *qs), row("ZF", *qs), row("ZI", *qs),
                cols[0:1, 16 + q:17 + q], Alu.add, Alu.max),
                r=[rkey("ZF"), rkey("ZI"), "mprev"], w=[rkey("MSC")], fragile=True)
            dve(lambda e, q=q, qs=qs: e.tensor_tensor_scan(
                row("FF", *qs), onesr[:, 0:128], row("ZF", *qs),
                0.0, Alu.mult, Alu.add),
                r=[rkey("ZF"), "onesr"], w=[rkey("FF")], fragile=True)
        dve(lambda e: e.tensor_tensor(row("WW"), row("FF"), row("MSC"), Alu.subtract),
            r=[rkey("FF"), rkey("MSC")], w=[rkey("WW")])
        dve(lambda e: e.tensor_tensor(row("UU"), row("ZI"), row("FF"), Alu.subtract),
            r=[rkey("ZI"), rkey("FF")], w=[rkey("UU")])
        for q in range(NSEQ):
            qs = (q * 128, (q + 1) * 128)
            dve(lambda e, q=q, qs=qs: e.tensor_scalar(
                row("ZI", *qs), row("WW", *qs), cols[0:1, 16 + q:17 + q],
                None, Alu.add),
                r=[rkey("WW"), "mprev"], w=[rkey("ZI")])
            dve(lambda e, q=q, qs=qs: e.tensor_scalar(
                row("ZF", *qs), row("UU", *qs), row("WW", qs[1] - 1, qs[1]),
                None, Alu.add),
                r=[rkey("UU"), rkey("WW")], w=[rkey("ZF")])
        for q in range(NSEQ):
            dve(lambda e, q=q: e.tensor_copy(
                cols[0:1, 16 + q:17 + q], row("MSC", (q + 1) * 128 - 1, (q + 1) * 128)),
                r=[rkey("MSC")], w=["mprev"])
        act(lambda e: e.activation(row("ALPHA"), row("ZI"), AF.Exp),
            r=[rkey("ZI")], w=[rkey("ALPHA")])
        act(lambda e: e.activation(row("BETA"), row("ZF"), AF.Exp),
            r=[rkey("ZF")], w=[rkey("BETA")])

        qlist = [int(c) for c in os.environ.get("KSEQ", "0123")]
        KAB = os.environ.get("KAB", "abcdefgijk")
        for q in qlist:
            qs0 = q * 128
            if "a" in KAB:
                for mc in range(MCN):
                    mm(p[2][:], nrm[:, mc * 512 + qs0:mc * 512 + qs0 + 128],
                       wv[:, mc * 512:(mc + 1) * 512],
                       r=[f"nrm{mc}", "wv"], w=["P2"], start=(mc == 0), stop=(mc == MCN - 1))
                act(lambda e: e.activation(vtok[:], p[2][:], AF.Copy),
                    r=["P2"], w=["vtok"])
                for mc in range(MCN):
                    mm(p[3][:], nrm[:, mc * 512 + qs0:mc * 512 + qs0 + 128],
                       wo[:, mc * 512:(mc + 1) * 512],
                       r=[f"nrm{mc}", "wo"], w=["P3"], start=(mc == 0), stop=(mc == MCN - 1))
                dve(lambda e: e.scalar_tensor_tensor(ztmp[:], p[3][:], 1.0, bo_bc[:],
                                                     Alu.mult, Alu.add),
                    r=["P3", "bo_bc"], w=["ztmp"])
                act(lambda e: e.activation(otok[:], ztmp[:], AF.Sigmoid),
                    r=["ztmp"], w=["otok"])
            else:
                dve(lambda e: e.memset(vtok[:], 0.01), r=["P2"], w=["vtok"])
                dve(lambda e: e.memset(otok[:], 0.5), r=["P3"], w=["otok"])
            if "b" in KAB:
                col_of(row("UU", qs0, qs0 + 128), p[2][:, 0:1], [rkey("UU")], "P2")
                dve(lambda e: e.tensor_copy(cols[:, CU:CU + 1], p[2][:, 0:1]),
                    r=["P2"], w=["colsu"])
                col_of(row("ALPHA", qs0, qs0 + 128), p[2][:, 1:2], [rkey("ALPHA")], "P2")
                dve(lambda e: e.tensor_copy(cols[:, CA:CA + 1], p[2][:, 1:2]),
                    r=["P2"], w=["colsa"])
                col_of(row("BETA", qs0, qs0 + 128), p[2][:, 2:3], [rkey("BETA")], "P2")
                dve(lambda e: e.tensor_copy(cols[:, CB:CB + 1], p[2][:, 2:3]),
                    r=["P2"], w=["colsb"])
                bcast(row("ALPHA", qs0 + 127, qs0 + 128), p[2][:, 3:4], [rkey("ALPHA")], "P2")
                dve(lambda e: e.tensor_copy(cols[:, CL:CL + 1], p[2][:, 3:4]),
                    r=["P2"], w=["colsl"])
            else:
                dve(lambda e: e.memset(cols[:, 0:8], 0.25),
                    r=["P2", "colsu", "colsa", "colsb", "colsl"],
                    w=["colsu", "colsa", "colsb", "colsl", "colsd", "colsd2", "colsr"])
            if "c" in KAB:
                bcast(row("WW", qs0, qs0 + 128), p[3][:, 0:128], [rkey("WW")], "P3")
                dve(lambda e: e.scalar_tensor_tensor(ztmp[:, 0:128], p[3][:, 0:128],
                                                     cols[:, CU:CU + 1], maskn[:],
                                                     Alu.add, Alu.add),
                    r=["P3", "colsu", "maskn"], w=["ztmp"])
                act(lambda e: e.activation(sbf[:, 640:768], ztmp[:, 0:128], AF.Exp),
                    r=["ztmp"], w=["dsb"])
            else:
                dve(lambda e: e.memset(sbf[:, 640:768], 0.125), r=["P3"], w=["dsb"])
            if "d" in KAB:
                for dc in range(DCN):
                    act(lambda e, dc=dc: e.activation(
                        sbf[:, dc * 128:(dc + 1) * 128],
                        qsb[:, dc * 512 + qs0:dc * 512 + qs0 + 128], AF.Copy),
                        r=["qsb"], w=["qbf"])
            else:
                dve(lambda e: e.memset(sbf[:, 0:512], 0.03), r=["qsb"], w=["qbf"])
            if "e" in KAB:
                for dc in range(DCN):
                    mm(p[2][:, 0:128], ksb[:, dc * 512 + qs0:dc * 512 + qs0 + 128],
                       sbf[:, dc * 128:(dc + 1) * 128],
                       r=["ksb", "qbf"], w=["P2"], start=(dc == 0), stop=(dc == DCN - 1))
                dve(lambda e: e.tensor_tensor(sbf[:, 512:640], sbf[:, 640:768],
                                              p[2][:, 0:128], Alu.mult),
                    r=["dsb", "P2"], w=["stl"])
            else:
                dve(lambda e: e.memset(sbf[:, 512:640], 0.02),
                    r=["dsb", "P2", "ksb", "qbf"], w=["stl"])
            if "g" in KAB:
                for jc in range(DCN):
                    mm(p[1][:], qsb[:, jc * 512 + qs0:jc * 512 + qs0 + 128],
                       CT[:, q * 2048 + jc * 512:q * 2048 + (jc + 1) * 512],
                       r=["qsb", "CT"], w=["P1"], start=(jc == 0), stop=(jc == DCN - 1))
                for jc in range(DCN):
                    mm(p[3][:, 131:132], qsb[:, jc * 512 + qs0:jc * 512 + qs0 + 128],
                       nst[:, q * 4 + jc:q * 4 + jc + 1],
                       r=["qsb", "nst"], w=["P3"], start=(jc == 0), stop=(jc == DCN - 1))
            else:
                dve(lambda e: e.memset(p[1][:], 0.02), r=["qsb", "CT", "P1"], w=["P1"])
                dve(lambda e: e.memset(p[3][:, 131:132], 0.1), r=["P3", "nst"], w=["P3"])
            if "i" in KAB:
                for mc in range(MCN):
                    mm(p[4][:], nrm[:, mc * 512 + qs0:mc * 512 + qs0 + 128],
                       wk[:, mc * 512:(mc + 1) * 512],
                       r=[f"nrm{mc}", "wk"], w=["P4"], start=(mc == 0), stop=(mc == MCN - 1))
                dve(lambda e: e.tensor_scalar(ktok[:], p[4][:], cols[:, CB:CB + 1],
                                              None, Alu.mult),
                    r=["P4", "colsb"], w=["ktok"])
            else:
                dve(lambda e: e.memset(ktok[:], 0.02),
                    r=["nrm", "wk", "P4", "colsb"], w=["ktok"])
            if "f" in KAB:
                mm(p[0][:], sbf[:, 512:640], vtok[:], r=["stl", "vtok"], w=["P0"])
                mm(p[3][:, 130:131], sbf[:, 512:640], onescb[:],
                   r=["stl", "onescb"], w=["P3"])
            else:
                dve(lambda e: e.memset(p[0][:], 0.02), r=["stl", "vtok", "P0"], w=["P0"])
                dve(lambda e: e.memset(p[3][:, 130:131], 0.7), r=["P3"], w=["P3"])
            if dbg_this and q == qlist[-1] and "2" in os.environ.get("KD", "1234"):
                dve(lambda e: e.tensor_copy(ztmp[:], p[0][:]), r=["P0"], w=["ztmp"])
                dma("act", ddbg_st[:, 512:1024], ztmp[:], r=["ztmp"], w=["dbgout"])
                dve(lambda e: e.tensor_copy(ztmp[:], p[1][:]), r=["P1"], w=["ztmp"])
                dma("act", ddbg_st[:, 1024:1536], ztmp[:], r=["ztmp"], w=["dbgout"])
                dve(lambda e: e.tensor_copy(ztmp[:, 0:8], p[3][:, 128:136]),
                    r=["P3"], w=["ztmp"])
                dma("act", ddbg_st[:, 128:136], ztmp[:, 0:8], r=["ztmp"], w=["dbgout"])
            for jc in (range(DCN) if "j" in KAB else ()):
                pp = 4 + (jc % 2)
                mm(p[pp][:], ktok[:, jc * 128:(jc + 1) * 128], vtok[:],
                   r=["ktok", "vtok"], w=[PK[pp]])
                if dbg_this and q == qlist[-1] and jc == 0 and "4" in os.environ.get("KD", "1234"):
                    dve(lambda e, pp=pp: e.tensor_copy(ztmp[:], p[pp][:]),
                        r=[PK[pp]], w=["ztmp"])
                    dma("act", ddbg_st[:, 2560:3072], ztmp[:], r=["ztmp"], w=["dbgout"])
                dve(lambda e, jc=jc, pp=pp, q=q: e.scalar_tensor_tensor(
                    CT[:, q * 2048 + jc * 512:q * 2048 + (jc + 1) * 512],
                    CT[:, q * 2048 + jc * 512:q * 2048 + (jc + 1) * 512],
                    cols[:, CL:CL + 1], p[pp][:], Alu.mult, Alu.add),
                    r=["CT", "colsl", PK[pp]], w=["CT"])
                mm(p[3][:, 132 + jc:133 + jc], ktok[:, jc * 128:(jc + 1) * 128],
                   onescb[:], r=["ktok", "onescb"], w=["P3"])
                dve(lambda e, jc=jc, q=q: e.scalar_tensor_tensor(
                    nst[:, q * 4 + jc:q * 4 + jc + 1],
                    nst[:, q * 4 + jc:q * 4 + jc + 1],
                    cols[:, CL:CL + 1], p[3][:, 132 + jc:133 + jc], Alu.mult, Alu.add),
                    r=["nst", "colsl", "P3"], w=["nst"])
            dve(lambda e: e.tensor_scalar(ztmp[:], p[1][:], cols[:, CA:CA + 1],
                                          None, Alu.mult),
                r=["P1", "colsa"], w=["ztmp"])
            dve(lambda e: e.tensor_tensor(ztmp[:], ztmp[:], p[0][:], Alu.add),
                r=["ztmp", "P0"], w=["ztmp"])
            dve(lambda e: e.tensor_scalar(cols[:, CD:CD + 1], p[3][:, 131:132],
                                          cols[:, CA:CA + 1], None, Alu.mult),
                r=["P3", "colsa"], w=["colsd"])
            dve(lambda e: e.tensor_tensor(cols[:, CD:CD + 1], cols[:, CD:CD + 1],
                                          p[3][:, 130:131], Alu.add),
                r=["colsd", "P3"], w=["colsd"])
            dve(lambda e: e.scalar_tensor_tensor(cols[:, CD2:CD2 + 1],
                                                 cols[:, CD:CD + 1], -1.0,
                                                 cols[:, CD:CD + 1], Alu.mult, Alu.max),
                r=["colsd"], w=["colsd2"])
            dve(lambda e: e.tensor_scalar(cols[:, CD2:CD2 + 1], cols[:, CD2:CD2 + 1],
                                          1.0, None, Alu.max),
                r=["colsd2"], w=["colsd2"])
            dve(lambda e: e.reciprocal(cols[:, CR:CR + 1], cols[:, CD2:CD2 + 1]),
                r=["colsd2"], w=["colsr"], fragile=True)
            dve(lambda e: e.tensor_tensor(ztmp[:], ztmp[:], otok[:], Alu.mult),
                r=["ztmp", "otok"], w=["ztmp"])
            dve(lambda e: e.tensor_scalar(htok[:], ztmp[:], cols[:, CR:CR + 1],
                                          None, Alu.mult),
                r=["ztmp", "colsr"], w=["htok"])
            for dc in (range(DCN) if "k" in KAB else ()):
                pp = dc % 2
                trans(p[pp][:, 0:128], htok[:, dc * 128:(dc + 1) * 128], identf[:],
                      r=["htok", "identf"], w=[PK[pp]])
                dve(lambda e, pp=pp, dc=dc: e.tensor_copy(
                    ztmp[:, dc * 128:(dc + 1) * 128], p[pp][:, 0:128]),
                    r=[PK[pp]], w=["ztmp"])
                dve(lambda e, dc=dc, q=q: e.tensor_copy(
                    ksb[:, dc * 512 + qs0:dc * 512 + qs0 + 128],
                    ztmp[:, dc * 128:(dc + 1) * 128]),
                    r=["ztmp"], w=["ksb"])

        for _d in range(8):
            mm(p[6][0:1, 0:1], onescb[:], onescb[:],
               r=["onescb"], w=["P6"])
        piece, pk = ring_fetch(dwout.rearrange("p a f -> p (a f)"), "dwout")
        for mc in range(MCN):
            pp = mc % 2
            for dc in range(DCN):
                mm(p[pp][:], piece[:, dc * 1024 + mc * 128:dc * 1024 + (mc + 1) * 128],
                   ksb[:, dc * 512:(dc + 1) * 512],
                   r=[pk, "ksb"], w=[PK[pp]], start=(dc == 0), stop=(dc == DCN - 1))
            dve(lambda e, mc=mc, pp=pp: e.tensor_tensor(
                x_sb[:, mc * 512:(mc + 1) * 512], x_sb[:, mc * 512:(mc + 1) * 512],
                p[pp][:], Alu.add),
                r=["x_sb", PK[pp]], w=["x_sb"])

    def dump_debug():
        def dump_f32(dst, src_ap, key):
            dma("act", dst, src_ap, r=[key], w=["dbgout"])
        def dump_bf(dst, src_ap, key, n):
            for c0 in range(0, n, 512):
                dve(lambda e, c0=c0, s=src_ap: e.tensor_copy(
                    ztmp[:, 0:512], s[:, c0:c0 + 512]), r=[key], w=["ztmp"])
                dma("act", dst[:, c0:c0 + 512], ztmp[:], r=["ztmp"], w=["dbgout"])
        dump_f32(ddbg_rb, rb[:], "R0")
        dump_f32(ddbg_cols, cols[:], "colsr")
        dump_f32(ddbg_qsb, qsb[:], "qsb")
        dump_f32(ddbg_ct, CT[:], "CT")
        dump_f32(ddbg_nst, nst[:], "nst")
        dump_bf(ddbg_ksb, ksb, "ksb", 2048)
        dump_bf(ddbg_htok, htok, "htok", 512)
        dump_bf(ddbg_vtok, vtok, "vtok", 512)
        dump_bf(ddbg_otok, otok, "otok", 512)
        dump_bf(ddbg_nrm, nrm, "nrm", 4096)
        dump_bf(ddbg_sbf, sbf, "qbf", 1024)

    load_x_group(0)
    for g in range(NG):
        ln_to_nrm()
        ffn_sweep()
        if stage >= 2:
            ln2_attn_in()
            attn_group(g, dbg_this=(dbg and g == 0))
            if dbg and g == 0:
                dump_debug()
        if stage >= 3:
            ln_to_nrm()
            ffn_sweep()
        store_x_group(g)
        if g + 1 < NG:
            load_x_group(g + 1)

    A.resolve()
    fin = A.final

    with es:
        sem_cm = ExitStack()
        sems = {}
        for nm in ("pe", "dve", "act", "sp", "pool"):
            sems[nm] = sem_cm.enter_context(nc.semaphore(f"s_{nm}"))
        for qq in ("sp", "act", "pool"):
            for ln in range(Asm.NLANE):
                sems[f"{qq}_dma{ln}"] = sem_cm.enter_context(
                    nc.semaphore(f"s_{qq}d{ln}"))
        with sem_cm, nc.Block() as blk:

            @blk.tensor
            def _(e):
                A.play("pe", e, sems)

            @blk.vector
            def _(e):
                A.play("dve", e, sems)

            @blk.scalar
            def _(e):
                A.play("act", e, sems)
                for ln in range(Asm.NLANE):
                    k = f"act_dma{ln}"
                    if k in fin:
                        e.wait_ge(sems[k], fin[k])

            @blk.sync
            def _(e):
                A.play("sp", e, sems)
                for ln in range(Asm.NLANE):
                    k = f"sp_dma{ln}"
                    if k in fin:
                        e.wait_ge(sems[k], fin[k])

            @blk.gpsimd
            def _(e):
                A.play("pool", e, sems)
                for ln in range(Asm.NLANE):
                    k = f"pool_dma{ln}"
                    if k in fin:
                        e.wait_ge(sems[k], fin[k])

    return nc


def make_in_maps(L0_attn_x, L0_attn_norm_w, L0_attn_Wq, L0_attn_Wk, L0_attn_Wv,
                 L0_attn_wi_w, L0_attn_wi_b, L0_attn_wf_w, L0_attn_wf_b,
                 L0_attn_Wo, L0_attn_bo, L0_attn_Wout,
                 L0_ffn_k0, L0_ffn_k1, L0_ffn_k2, L0_ffn_s0, L0_ffn_s1, L0_ffn_s2):
    x = np.asarray(L0_attn_x, dtype=np.float32)
    asf = lambda a: np.ascontiguousarray(np.asarray(a, dtype=np.float32))
    wqT = asf(np.asarray(L0_attn_Wq).T)
    wkT = asf(np.asarray(L0_attn_Wk).T)
    wvT = asf(np.asarray(L0_attn_Wv).T)
    woT = asf(np.asarray(L0_attn_Wo).T)
    woutT = asf(np.asarray(L0_attn_Wout).T)
    nwr = asf(np.asarray(L0_attn_norm_w).reshape(8, 128).T)
    wir = asf(np.asarray(L0_attn_wi_w).reshape(8, 128).T)
    wfr = asf(np.asarray(L0_attn_wf_w).reshape(8, 128).T)
    bor = asf(np.asarray(L0_attn_bo).reshape(1, DH))
    scal = np.zeros((1, 8), np.float32)
    scal[0, 0] = np.float32(np.asarray(L0_ffn_s0))
    scal[0, 1] = np.float32(np.asarray(L0_ffn_s1))
    scal[0, 2] = np.float32(np.asarray(L0_ffn_s2))
    scal[0, 3] = np.float32(np.asarray(L0_attn_wi_b).reshape(-1)[0])
    scal[0, 4] = np.float32(np.asarray(L0_attn_wf_b).reshape(-1)[0])
    k0u = np.ascontiguousarray(np.asarray(L0_ffn_k0)).view(np.uint8)
    k1u = np.ascontiguousarray(np.asarray(L0_ffn_k1)).view(np.uint8)
    k2u = np.ascontiguousarray(np.asarray(L0_ffn_k2)).view(np.uint8)
    idb = np.eye(128, dtype=BFD)
    idf = np.eye(128, dtype=np.float32)
    mskn = np.where(np.arange(128)[:, None] <= np.arange(128)[None, :],
                    np.float32(0.0), np.float32(NEGBIG)).astype(np.float32)
    in_maps = []
    for c in range(NCORES):
        in_maps.append({
            "x": np.ascontiguousarray(x[c * NSEQ:(c + 1) * NSEQ]),
            "wqT": wqT, "wkT": wkT, "wvT": wvT, "woT": woT, "woutT": woutT,
            "nwr": nwr, "wir": wir, "wfr": wfr, "bor": bor, "scal": scal,
            "k0u": k0u, "k1u": k1u, "k2u": k2u,
            "idb": idb, "idf": idf, "mskn": mskn,
        })
    return in_maps


def kernel(**inputs):
    in_maps = make_in_maps(**inputs)
    nc = build_program()
    res = run_bass_kernel_spmd(nc, in_maps, core_ids=list(range(NCORES)))
    out = np.concatenate([np.asarray(r["out"]) for r in res.results], axis=0)
    return out.astype(np.float32)


# version 10
# speedup vs baseline: 1.0828x; 1.0828x over previous
import os
import numpy as np
import ml_dtypes
from contextlib import ExitStack

import concourse.bass as bass
import concourse.mybir as mybir
from concourse.bass_utils import run_bass_kernel_spmd
from concourse.alu_op_type import AluOpType as Alu

F32 = mybir.dt.float32
BF16 = mybir.dt.bfloat16
U8 = mybir.dt.uint8
AF = mybir.ActivationFunctionType
BFD = ml_dtypes.bfloat16

NCORES = 8
D = 1024
DH = 512
T = 2048
NSEQ = 4
L = 128
NG = T // L
MCN = 8
DCN = 4
NSLOT = 4
EPS_LN = 1e-5
EPS_RMS = 1e-6
SCALE_Q = float(np.float32(1.0) / np.sqrt(np.float32(DH)))
NEGBIG = -1e30

STAGE = 3




class Asm:
    ENGS = ("pe", "dve", "act", "sp", "pool")
    NLANE = 6

    def __init__(self):
        self.ops = []

    def op(self, eng, fn, r=(), w=(), dma=False, fragile=False):
        o = {"e": eng, "fn": fn, "r": tuple(r), "w": tuple(w), "dma": dma,
             "i": len(self.ops), "inc": bool(dma), "needs": {}, "lane": None,
             "fragile": fragile}
        self.ops.append(o)
        return o

    def sem_key(self, o):
        if o["dma"]:
            return f"{o['e']}_dma{o['lane']}"
        return o["e"]

    def resolve(self):
        lane_ctr = {}
        for o in self.ops:
            if o["dma"]:
                c = lane_ctr.get(o["e"], 0)
                o["lane"] = c % self.NLANE
                lane_ctr[o["e"]] = c + 1
        lastw = {}
        readers = {}
        for o in self.ops:
            needs = {}

            def need(t):
                if (not t["dma"]) and t["e"] == o["e"] and not o["fragile"]:
                    return
                k = self.sem_key(t)
                prev = needs.get(k)
                if prev is None or t["i"] > prev["i"]:
                    needs[k] = t

            for k in o["r"]:
                t = lastw.get(k)
                if t is not None:
                    need(t)
            for k in o["w"]:
                t = lastw.get(k)
                if t is not None:
                    need(t)
                for t2 in readers.get(k, ()):
                    need(t2)
            o["needs"] = needs
            for k in o["r"]:
                readers.setdefault(k, []).append(o)
            for k in o["w"]:
                lastw[k] = o
                readers[k] = []
        for o in self.ops:
            for t in o["needs"].values():
                t["inc"] = True
        cum = {}
        prev_lane_cum = {}
        for o in self.ops:
            sk = self.sem_key(o)
            if o["dma"]:
                o["gate"] = cum.get(sk, 0)
            if o["inc"]:
                cum[sk] = cum.get(sk, 0) + (16 if o["dma"] else 1)
            o["cum"] = cum.get(sk, 0)
        self.final = dict(cum)
        maxw = {e: {} for e in self.ENGS}
        for o in self.ops:
            o["waits"] = []
            gates = []
            if o["dma"] and o.get("gate", 0) > 0:
                gates.append((self.sem_key(o), o["gate"]))
            for sk, v in gates + sorted(
                    [(sk, t["cum"]) for sk, t in o["needs"].items()]):
                if v > maxw[o["e"]].get(sk, 0):
                    o["waits"].append((sk, v))
                    maxw[o["e"]][sk] = v

    hop = {}

    def play(self, eng_name, eng, sems):
        import bass_rust as _br
        DI = _br.DependencyInfo(_br.DependencyClass.Sync,
                                _br.DependencyReason.DATA)
        last = None

        def chain(bi):
            nonlocal last
            try:
                ins = bi.ins
            except Exception:
                return
            if last is not None:
                try:
                    ins.add_dependency(last, DI)
                except Exception:
                    pass
            last = ins.name

        for o in self.ops:
            if o["e"] != eng_name:
                continue
            for sk, v in o["waits"]:
                w = eng.wait_ge(sems[sk], v)
                if w is not None:
                    chain(w)
            inst = o["fn"](eng)
            chain(inst)
            if o["inc"]:
                h = None if o["dma"] else self.hop.get(o["e"])
                if h is not None:
                    h2 = h(eng)
                    chain(h2)
                    h2.then_inc(sems[o["e"]], 1)
                else:
                    inst.then_inc(sems[self.sem_key(o)], 16 if o["dma"] else 1)


def build_program(stage=STAGE, dbg=False):
    nc = bass.Bass("TRN2", target_bir_lowering=False)
    A = Asm()

    dx = nc.dram_tensor("x", [NSEQ, T, D], F32, kind="ExternalInput").ap()
    dout = nc.dram_tensor("out", [NSEQ, T, D], F32, kind="ExternalOutput").ap()
    dwqT = nc.dram_tensor("wqT", [D, DH], F32, kind="ExternalInput").ap()
    dwkT = nc.dram_tensor("wkT", [D, DH], F32, kind="ExternalInput").ap()
    dwvT = nc.dram_tensor("wvT", [D, DH], F32, kind="ExternalInput").ap()
    dwoT = nc.dram_tensor("woT", [D, DH], F32, kind="ExternalInput").ap()
    dwoutT = nc.dram_tensor("woutT", [DH, D], F32, kind="ExternalInput").ap()
    dnw = nc.dram_tensor("nwr", [128, 8], F32, kind="ExternalInput").ap()
    dwi = nc.dram_tensor("wir", [128, 8], F32, kind="ExternalInput").ap()
    dwf = nc.dram_tensor("wfr", [128, 8], F32, kind="ExternalInput").ap()
    dbo = nc.dram_tensor("bor", [1, DH], F32, kind="ExternalInput").ap()
    dsc = nc.dram_tensor("scal", [1, 8], F32, kind="ExternalInput").ap()
    dk0 = nc.dram_tensor("k0u", [2 * D, 4096], U8, kind="ExternalInput").ap()
    dk1 = nc.dram_tensor("k1u", [4096, 4096], U8, kind="ExternalInput").ap()
    dk2 = nc.dram_tensor("k2u", [4096, D], U8, kind="ExternalInput").ap()
    didb = nc.dram_tensor("idb", [128, 128], BF16, kind="ExternalInput").ap()
    didf = nc.dram_tensor("idf", [128, 128], F32, kind="ExternalInput").ap()
    dmask = nc.dram_tensor("mskn", [128, 128], F32, kind="ExternalInput").ap()
    if dbg:
        ddbg_rb = nc.dram_tensor("dbg_rb", [1, 3072], F32, kind="ExternalOutput").ap()
        ddbg_cols = nc.dram_tensor("dbg_cols", [128, 24], F32, kind="ExternalOutput").ap()
        ddbg_ksb = nc.dram_tensor("dbg_ksb", [128, 2048], F32, kind="ExternalOutput").ap()
        ddbg_qsb = nc.dram_tensor("dbg_qsb", [128, 2048], F32, kind="ExternalOutput").ap()
        ddbg_htok = nc.dram_tensor("dbg_htok", [128, 512], F32, kind="ExternalOutput").ap()
        ddbg_vtok = nc.dram_tensor("dbg_vtok", [128, 512], F32, kind="ExternalOutput").ap()
        ddbg_otok = nc.dram_tensor("dbg_otok", [128, 512], F32, kind="ExternalOutput").ap()
        ddbg_ct = nc.dram_tensor("dbg_ct", [128, 8192], F32, kind="ExternalOutput").ap()
        ddbg_nst = nc.dram_tensor("dbg_nst", [128, 16], F32, kind="ExternalOutput").ap()
        ddbg_nrm = nc.dram_tensor("dbg_nrm", [128, 4096], F32, kind="ExternalOutput").ap()
        ddbg_sbf = nc.dram_tensor("dbg_sbf", [128, 1024], F32, kind="ExternalOutput").ap()
        ddbg_k2 = nc.dram_tensor("dbg_k2", [128, 2048], F32, kind="ExternalOutput").ap()
        ddbg_kt = nc.dram_tensor("dbg_kt", [128, 512], F32, kind="ExternalOutput").ap()
        ddbg_s3 = nc.dram_tensor("dbg_s3", [128, 128], F32, kind="ExternalOutput").ap()
        ddbg_d3 = nc.dram_tensor("dbg_d3", [128, 128], F32, kind="ExternalOutput").ap()
        ddbg_st = nc.dram_tensor("dbg_st", [128, 4096], F32, kind="ExternalOutput").ap()
    dw0 = nc.dram_tensor("w0s", [8, 128, 4096], BF16, kind="Internal").ap()
    dw1 = nc.dram_tensor("w1s", [32, 128, 4096], BF16, kind="Internal").ap()
    dw2 = nc.dram_tensor("w2s", [32, 128, 1024], BF16, kind="Internal").ap()
    dwout = nc.dram_tensor("wouts", [128, 4, 1024], BF16, kind="Internal").ap()

    es = ExitStack()
    sb = lambda n, s, d: es.enter_context(nc.sbuf_tensor(n, s, d))
    ps = lambda n, s, d: es.enter_context(nc.psum_tensor(n, s, d))

    x_sb = sb("x_sb", [128, 4096], F32)
    xin = sb("xin", [128, 1024], F32)
    xout = sb("xout", [128, 1024], F32)
    nrm = sb("nrm", [128, 4096], BF16)
    h0 = sb("h0", [128, 16384], BF16)
    h1r = sb("h1r", [128, 4096], BF16)
    qsb = sb("qsb", [128, 2048], F32)
    ksb = sb("ksb", [128, 2048], BF16)
    sbf = sb("sbf", [128, 1024], BF16)
    ktok = sb("ktok", [128, 512], BF16)
    vtok = sb("vtok", [128, 512], BF16)
    otok = sb("otok", [128, 512], BF16)
    htok = sb("htok", [128, 512], F32)
    ztmp = sb("ztmp", [128, 512], F32)
    CT = sb("CT", [128, 8192], F32)
    nst = sb("nst", [128, 16], F32)
    wring = sb("wring", [128, NSLOT * 4096], BF16)
    rb = sb("rb", [1, 3072], F32)
    rbb = sb("rbb", [1, 1024], BF16)
    cols = sb("cols", [128, 24], F32)
    params = sb("params", [128, 8], F32)
    nwsb = sb("nwsb", [128, 8], F32)
    wisb = sb("wisb", [128, 8], F32)
    wfsb = sb("wfsb", [128, 8], F32)
    wib_b = sb("wib_b", [128, 8], BF16)
    wfb_b = sb("wfb_b", [128, 8], BF16)
    wq = sb("wq", [128, 4096], BF16)
    wk = sb("wk", [128, 4096], BF16)
    wv = sb("wv", [128, 4096], BF16)
    wo = sb("wo", [128, 4096], BF16)
    bo_bc = sb("bo_bc", [128, 512], BF16)
    identf = sb("identf", [128, 128], F32)
    maskn = sb("maskn", [128, 128], F32)
    onesr = sb("onesr", [1, 128], F32)
    onescb = sb("onescb", [128, 1], BF16)

    p = [ps(f"p{i}", [128, 512], F32) for i in range(7)]
    psync = ps("psync", [128, 512], F32)
    PK = [f"P{i}" for i in range(7)] + ["PT"]

    HOME = {"MEAN": 0, "EX2": 1, "VAR": 2, "STD": 3, "RSTD": 4, "MR": 5,
            "ZI": 0, "ZF": 1, "MSC": 2, "FF": 3, "WW": 4, "UU": 5,
            "ALPHA": 2, "BETA": 3}

    def row(nm, c0=0, c1=512):
        j = HOME[nm]
        return rb[0:1, j * 512 + c0:j * 512 + c1]

    def rkey(nm):
        return f"R{HOME[nm]}"

    CU, CA, CB, CL, CD, CD2, CR = 0, 1, 2, 3, 4, 5, 6

    dum = cols
    import os as _os
    _hops = _os.environ.get("KHOP", "")
    if "p" in _hops:
        A.hop["pe"] = lambda e: e.matmul(psync[0:1, 0:1], onescb[:, 0:1],
                                         onescb[:, 0:1], start=True, stop=True)
    if "d" in _hops:
        A.hop["dve"] = lambda e: e.tensor_scalar(dum[0:1, 20:21], dum[0:1, 21:22],
                                                 1.0, None, Alu.mult)
    if "a" in _hops:
        A.hop["act"] = lambda e: e.activation(dum[0:1, 20:21], dum[0:1, 21:22], AF.Copy)

    def mm(out_ap, lhsT, rhs, r, w, start=True, stop=True):
        return A.op("pe", lambda e, o=out_ap, l=lhsT, rr=rhs, st=start, sp_=stop:
                    e.matmul(o, l, rr, start=st, stop=sp_), r=r, w=w)

    def trans(out_ap, in_ap, ident, r, w):
        return A.op("pe", lambda e, o=out_ap, i=in_ap, d=ident:
                    e.transpose(o, i, d), r=r, w=w)

    def dve(fn, r, w, fragile=False):
        return A.op("dve", fn, r=r, w=w, fragile=fragile)

    def act(fn, r, w):
        return A.op("act", fn, r=r, w=w)

    def dma(eng, out_ap, in_ap, r, w):
        return A.op(eng, lambda e, o=out_ap, i=in_ap: e.dma_start(out=o, in_=i),
                    r=r, w=w, dma=True)

    def bcast(row_ap, out_psum, rkeys, pkey, n=None):
        return mm(out_psum, onesr[:, 0:128], row_ap,
                  r=list(rkeys) + ["onesr"], w=[pkey])

    def col_of(row_ap, out_psum, rkeys, pkey):
        return mm(out_psum, row_ap, onesr[:, 0:1],
                  r=list(rkeys) + ["onesr"], w=[pkey])

    dve(lambda e: e.memset(onesr[:], 1.0), r=[], w=["onesr"])
    dve(lambda e: e.memset(onescb[:], 1.0), r=[], w=["onescb"])
    dve(lambda e: e.memset(CT[:], 0.0), r=[], w=["CT"])
    dve(lambda e: e.memset(nst[:], 0.0), r=[], w=["nst"])
    dve(lambda e: e.memset(cols[0:1, 16:20], NEGBIG), r=[], w=["mprev"])
    dve(lambda e: e.memset(cols[0:1, 21:22], EPS_LN), r=[], w=["epsln"])
    dve(lambda e: e.memset(cols[0:1, 22:23], EPS_RMS), r=[], w=["epsrm"])
    dma("sp", identf[:], didf, r=[], w=["identf"])
    dma("sp", maskn[:], dmask, r=[], w=["maskn"])
    dma("sp", nwsb[:], dnw, r=[], w=["nwsb"])
    dma("sp", wisb[:], dwi, r=[], w=["wisb"])
    dma("sp", wfsb[:], dwf, r=[], w=["wfsb"])
    dma("sp", cols[0:1, 8:16], dsc, r=[], w=["scal"])
    dma("sp", row("MR"), dbo, r=[], w=[rkey("MR")])
    dve(lambda e: e.tensor_tensor(wisb[:], wisb[:], nwsb[:], Alu.mult),
        r=["wisb", "nwsb"], w=["wisb"])
    dve(lambda e: e.tensor_tensor(wfsb[:], wfsb[:], nwsb[:], Alu.mult),
        r=["wfsb", "nwsb"], w=["wfsb"])
    dve(lambda e: e.tensor_copy(wib_b[:], wisb[:]), r=["wisb"], w=["wib_b"])
    dve(lambda e: e.tensor_copy(wfb_b[:], wfsb[:]), r=["wfsb"], w=["wfb_b"])
    for j, mult in ((0, 4.0), (1, 2.0), (2, 0.5)):
        dve(lambda e, j=j, m=mult: e.tensor_scalar(
            cols[0:1, 20:21], cols[0:1, 8 + j:9 + j], m, None, Alu.mult),
            r=["scal"], w=["c20"])
        bcast(cols[0:1, 20:21], p[6][:, 0:1], ["c20"], "P6")
        dve(lambda e, j=j: e.tensor_copy(params[:, j:j + 1], p[6][:, 0:1]),
            r=["P6"], w=["params"])
    bcast(row("MR"), p[5][:], [rkey("MR")], "P5")
    dve(lambda e: e.tensor_copy(bo_bc[:], p[5][:]), r=["P5"], w=["bo_bc"])

    for wsb, ddr, scl in ((wq, dwqT, SCALE_Q), (wk, dwkT, SCALE_Q),
                          (wv, dwvT, 1.0), (wo, dwoT, 1.0)):
        nm = wsb.name
        for mc in range(MCN):
            dma("sp", ztmp[:, 0:512], ddr[mc * 128:(mc + 1) * 128, :],
                r=[], w=["ztmp"])
            dve(lambda e, wsb=wsb, mc=mc, scl=scl: e.tensor_scalar(
                wsb[:, mc * 512:(mc + 1) * 512], ztmp[:, 0:512],
                nwsb[:, mc:mc + 1], scl, Alu.mult, Alu.mult),
                r=["ztmp", "nwsb"], w=[nm])
    for dc in range(DCN):
        for hh in range(2):
            dma("sp", ztmp[:, 0:512],
                dwoutT[dc * 128:(dc + 1) * 128, hh * 512:(hh + 1) * 512],
                r=[], w=["ztmp"])
            act(lambda e: e.activation(htok[:, 0:512], ztmp[:, 0:512], AF.Copy),
                r=["ztmp"], w=["htok"])
            dma("pool", dwout[:, dc, hh * 512:(hh + 1) * 512], htok[:, 0:512],
                r=["htok"], w=["dwout"])

    ha = lambda: h0[:, 0:4096]
    hb_ = lambda: h0[:, 4096:8192]
    hw = lambda: h0[:, 8192:12288]
    hw2 = lambda: h0[:, 12288:16384]
    PRE_KEYS = ["h0a", "h0b", "h0w", "h0w2"]
    for kc in range(8):
        wk_, wkk = (hw(), "h0w") if kc % 2 == 0 else (hw2(), "h0w2")
        dma("pool", ha(), dk0[kc * 128:(kc + 1) * 128, :], r=[], w=["h0a"])
        dma("pool", hb_(), dk0[D + kc * 128:D + (kc + 1) * 128, :], r=[], w=["h0b"])
        dve(lambda e, wk_=wk_: e.tensor_tensor(wk_, ha(), hb_(), Alu.subtract),
            r=["h0a", "h0b"], w=[wkk])
        dma("sp", dw0[kc], wk_, r=[wkk], w=["w0s"])
    for kc in range(32):
        ld, ldk = (ha(), "h0a") if kc % 2 == 0 else (hb_(), "h0b")
        wk_, wkk = (hw(), "h0w") if kc % 2 == 0 else (hw2(), "h0w2")
        dma("pool", ld, dk1[kc * 128:(kc + 1) * 128, :], r=[], w=[ldk])
        dve(lambda e, ld=ld, wk_=wk_: e.tensor_scalar(
            wk_, ld, 2.0, -1.0, Alu.mult, Alu.add), r=[ldk], w=[wkk])
        dma("sp", dw1[kc], wk_, r=[wkk], w=[f"w1s{kc // 8}"])
    for kc in range(32):
        ld, ldk = (ha(), "h0a") if kc % 2 == 0 else (hb_(), "h0b")
        wk_, wkk = (hw(), "h0w") if kc % 2 == 0 else (hw2(), "h0w2")
        dma("pool", ld[:, 0:1024], dk2[kc * 128:(kc + 1) * 128, :], r=[], w=[ldk])
        dve(lambda e, ld=ld, wk_=wk_: e.tensor_scalar(
            wk_[:, 0:1024], ld[:, 0:1024], 2.0, -1.0, Alu.mult, Alu.add),
            r=[ldk], w=[wkk])
        dma("sp", dw2[kc], wk_[:, 0:1024], r=[wkk], w=[f"w2s{kc // 8}"])

    ring_ctr = [0]

    def ring_fetch(src_ap, src_key, f3=None):
        s = ring_ctr[0] % NSLOT
        qq = "sp" if (ring_ctr[0] % 2 == 0) else "pool"
        ring_ctr[0] += 1
        slot = wring[:, s * 4096:(s + 1) * 4096]
        dst = slot.rearrange("p (k f) -> p k f", f=f3) if f3 else slot
        dma(qq, dst, src_ap, r=[src_key], w=[f"ring{s}"])
        return slot, f"ring{s}"

    first_stats = [True]

    def ln_stats():
        bridge = PRE_KEYS if first_stats[0] else []
        first_stats[0] = False
        for mc in range(MCN):
            act(lambda e, mc=mc: e.activation(
                ha()[:, mc * 512:(mc + 1) * 512], x_sb[:, mc * 512:(mc + 1) * 512],
                AF.Copy), r=["x_sb"], w=["h0"] + bridge)
            act(lambda e, mc=mc: e.activation(
                hb_()[:, mc * 512:(mc + 1) * 512], x_sb[:, mc * 512:(mc + 1) * 512],
                AF.Square), r=["x_sb"], w=["h0"])
        for mc in range(MCN):
            mm(p[5][0:1, :], onescb[:], ha()[:, mc * 512:(mc + 1) * 512],
               r=["onescb", "h0"], w=["P5"], start=(mc == 0), stop=(mc == MCN - 1))
        for mc in range(MCN):
            mm(p[6][0:1, :], onescb[:], hb_()[:, mc * 512:(mc + 1) * 512],
               r=["onescb", "h0"], w=["P6"], start=(mc == 0), stop=(mc == MCN - 1))
        dve(lambda e: e.tensor_scalar(row("MEAN"), p[5][0:1, :], 1.0 / D, None, Alu.mult),
            r=["P5"], w=[rkey("MEAN")])
        dve(lambda e: e.tensor_scalar(row("EX2"), p[6][0:1, :], 1.0 / D, None, Alu.mult),
            r=["P6"], w=[rkey("EX2")])
        dve(lambda e: e.tensor_tensor(row("STD"), row("MEAN"), row("MEAN"), Alu.mult),
            r=[rkey("MEAN")], w=[rkey("STD")])
        dve(lambda e: e.tensor_tensor(row("VAR"), row("EX2"), row("STD"), Alu.subtract),
            r=[rkey("EX2"), rkey("STD")], w=[rkey("VAR")])
        act(lambda e: e.activation(row("STD"), row("VAR"), AF.Sqrt,
                                   bias=cols[0:1, 21:22]),
            r=[rkey("VAR"), "epsln"], w=[rkey("STD")])
        dve(lambda e: e.reciprocal(row("RSTD"), row("STD")),
            r=[rkey("STD")], w=[rkey("RSTD")], fragile=True)

    sweep_no = [0]

    def ffn_sweep():
        first = (sweep_no[0] == 0)
        sweep_no[0] += 1
        for sbi in range(8):
            piece, pk = ring_fetch(
                dw0[:, :, sbi * 512:(sbi + 1) * 512].rearrange("k p f -> p k f"),
                "w0s", f3=512)
            for kc in range(8):
                for ocl in range(4):
                    mm(p[ocl][:], piece[:, kc * 512 + ocl * 128:kc * 512 + (ocl + 1) * 128],
                       nrm[:, kc * 512:(kc + 1) * 512],
                       r=[pk, f"nrm{kc}"], w=[PK[ocl]], start=(kc == 0), stop=(kc == 7))
            for ocl in range(4):
                oc = sbi * 4 + ocl
                act(lambda e, ocl=ocl, oc=oc: e.activation(
                    h0[:, oc * 512:(oc + 1) * 512], p[ocl][:], AF.Relu,
                    scale=params[:, 0:1]),
                    r=[PK[ocl], "params"], w=["h0"])
        for sbi in range(8):
            for pc in range(4):
                piece, pk = ring_fetch(
                    dw1[pc * 8:(pc + 1) * 8, :, sbi * 512:(sbi + 1) * 512]
                    .rearrange("k p f -> p k f"), f"w1s{pc}", f3=512)
                for kcl in range(8):
                    kc = pc * 8 + kcl
                    for ocl in range(4):
                        mm(p[ocl][:],
                           piece[:, kcl * 512 + ocl * 128:kcl * 512 + (ocl + 1) * 128],
                           h0[:, kc * 512:(kc + 1) * 512],
                           r=[pk, "h0"], w=[PK[ocl]],
                           start=(kc == 0), stop=(kc == 31))
            for ocl in range(4):
                oc = sbi * 4 + ocl
                slot = oc % 8
                act(lambda e, ocl=ocl, slot=slot: e.activation(
                    h1r[:, slot * 512:(slot + 1) * 512], p[ocl][:], AF.Relu,
                    scale=params[:, 1:2]),
                    r=[PK[ocl], "params"], w=[f"h1_{slot}"])
            if sbi % 2 == 1:
                rnd = sbi // 2
                for half in range(2):
                    piece, pk = ring_fetch(
                        dw2[rnd * 8:(rnd + 1) * 8, :, half * 512:(half + 1) * 512]
                        .rearrange("k p f -> p k f"), f"w2s{rnd}", f3=512)
                    for mcl in range(4):
                        mc = half * 4 + mcl
                        pp = 4 + (mc % 2)
                        for kcr in range(8):
                            mm(p[pp][:],
                               piece[:, kcr * 512 + mcl * 128:kcr * 512 + (mcl + 1) * 128],
                               h1r[:, kcr * 512:(kcr + 1) * 512],
                               r=[pk, f"h1_{kcr}"], w=[PK[pp]],
                               start=(kcr == 0), stop=(kcr == 7))
                        dve(lambda e, mc=mc, pp=pp: e.scalar_tensor_tensor(
                            x_sb[:, mc * 512:(mc + 1) * 512], p[pp][:],
                            params[:, 2:3], x_sb[:, mc * 512:(mc + 1) * 512],
                            Alu.mult, Alu.add),
                            r=[PK[pp], "params", "x_sb"], w=["x_sb"])

    def load_x_group(g):
        for q in range(NSEQ):
            dma("act", xin[:], dx[q, g * L:(g + 1) * L, :], r=[], w=["xin"])
            for mc in range(MCN):
                pp = mc % 2
                trans(p[pp][:, 0:128], xin[:, mc * 128:(mc + 1) * 128], identf[:],
                      r=["xin", "identf"], w=[PK[pp]])
                dve(lambda e, mc=mc, q=q, pp=pp: e.tensor_copy(
                    x_sb[:, mc * 512 + q * 128:mc * 512 + (q + 1) * 128],
                    p[pp][:, 0:128]),
                    r=[PK[pp]], w=["x_sb"])

    def store_x_group(g):
        for q in range(NSEQ):
            for mc in range(MCN):
                pp = mc % 2
                trans(p[pp][:, 0:128],
                      x_sb[:, mc * 512 + q * 128:mc * 512 + (q + 1) * 128], identf[:],
                      r=["x_sb", "identf"], w=[PK[pp]])
                dve(lambda e, mc=mc, pp=pp: e.tensor_copy(
                    xout[:, mc * 128:(mc + 1) * 128], p[pp][:, 0:128]),
                    r=[PK[pp]], w=["xout"])
            dma("act", dout[q, g * L:(g + 1) * L, :], xout[:], r=["xout"], w=["dout"])

    def ln_to_nrm():
        ln_stats()
        bcast(row("MEAN"), p[5][:], [rkey("MEAN")], "P5")
        bcast(row("RSTD"), p[6][:], [rkey("RSTD")], "P6")
        for mc in range(MCN):
            dve(lambda e, mc=mc: e.tensor_tensor(
                ztmp[:], x_sb[:, mc * 512:(mc + 1) * 512], p[5][:], Alu.subtract),
                r=["x_sb", "P5", "ztmp"], w=["ztmp"])
            dve(lambda e, mc=mc: e.tensor_tensor(
                nrm[:, mc * 512:(mc + 1) * 512], ztmp[:], p[6][:], Alu.mult),
                r=["ztmp", "P6"], w=[f"nrm{mc}"])

    def ln2_attn_in():
        ln_stats()
        dve(lambda e: e.tensor_scalar(row("EX2"), row("VAR"), EPS_LN, None, Alu.add),
            r=[rkey("VAR")], w=[rkey("EX2")])
        dve(lambda e: e.reciprocal(row("STD"), row("EX2")),
            r=[rkey("EX2")], w=[rkey("STD")], fragile=True)
        dve(lambda e: e.tensor_tensor(row("EX2"), row("VAR"), row("STD"), Alu.mult),
            r=[rkey("VAR"), rkey("STD")], w=[rkey("EX2")])
        act(lambda e: e.activation(row("STD"), row("EX2"), AF.Sqrt,
                                   bias=cols[0:1, 22:23]),
            r=[rkey("EX2"), "epsrm"], w=[rkey("STD")])
        dve(lambda e: e.reciprocal(row("UU"), row("STD")),
            r=[rkey("STD")], w=[rkey("UU")], fragile=True)
        bcast(row("MEAN"), p[6][:], [rkey("MEAN")], "P6")
        bcast(row("RSTD"), p[4][:], [rkey("RSTD")], "P4")
        bcast(row("UU"), p[5][:], [rkey("UU")], "P5")
        for mc in range(MCN):
            dve(lambda e, mc=mc: e.tensor_tensor(
                ztmp[:], x_sb[:, mc * 512:(mc + 1) * 512], p[6][:], Alu.subtract),
                r=["x_sb", "P6"], w=["ztmp"])
            dve(lambda e: e.tensor_tensor(ztmp[:], ztmp[:], p[4][:], Alu.mult),
                r=["ztmp", "P4"], w=["ztmp"])
            dve(lambda e, mc=mc: e.tensor_tensor(
                x_sb[:, mc * 512:(mc + 1) * 512], x_sb[:, mc * 512:(mc + 1) * 512],
                ztmp[:], Alu.add),
                r=["x_sb", "ztmp"], w=["x_sb"])
            dve(lambda e, mc=mc: e.tensor_tensor(
                nrm[:, mc * 512:(mc + 1) * 512], ztmp[:], p[5][:], Alu.mult),
                r=["ztmp", "P5"], w=[f"nrm{mc}"])

    def attn_group(g, dbg_this=False):
        for wsb, wkname, dest, destk in ((wq, "wq", qsb, "qsb"), (wk, "wk", ksb, "ksb")):
            for dc in range(DCN):
                pp = dc % 2
                for mc in range(MCN):
                    mm(p[pp][:],
                       wsb[:, mc * 512 + dc * 128:mc * 512 + (dc + 1) * 128],
                       nrm[:, mc * 512:(mc + 1) * 512],
                       r=[wkname, f"nrm{mc}"], w=[PK[pp]],
                       start=(mc == 0), stop=(mc == MCN - 1))
                act(lambda e, dest=dest, dc=dc, pp=pp: e.activation(
                    dest[:, dc * 512:(dc + 1) * 512], p[pp][:], AF.Copy),
                    r=[PK[pp]], w=[destk])
        if dbg_this:
            for c0 in range(0, 2048, 512):
                dve(lambda e, c0=c0: e.tensor_copy(ztmp[:, 0:512], ksb[:, c0:c0 + 512]),
                    r=["ksb"], w=["ztmp"])
                dma("act", ddbg_k2[:, c0:c0 + 512], ztmp[:], r=["ztmp"], w=["dbgout"])
        for mc in range(MCN):
            mm(p[2][0:1, :], wib_b[:, mc:mc + 1], nrm[:, mc * 512:(mc + 1) * 512],
               r=["wib_b", f"nrm{mc}"], w=["P2"], start=(mc == 0), stop=(mc == MCN - 1))
        for mc in range(MCN):
            mm(p[3][0:1, :], wfb_b[:, mc:mc + 1], nrm[:, mc * 512:(mc + 1) * 512],
               r=["wfb_b", f"nrm{mc}"], w=["P3"], start=(mc == 0), stop=(mc == MCN - 1))
        dve(lambda e: e.tensor_scalar(row("ZI"), p[2][0:1, :], cols[0:1, 11:12],
                                      None, Alu.add),
            r=["P2", "scal"], w=[rkey("ZI")])
        dve(lambda e: e.tensor_scalar(row("ZF"), p[3][0:1, :], cols[0:1, 12:13],
                                      None, Alu.add),
            r=["P3", "scal"], w=[rkey("ZF")])
        for q in range(NSEQ):
            qs = (q * 128, (q + 1) * 128)
            dve(lambda e, q=q, qs=qs: e.tensor_tensor_scan(
                row("MSC", *qs), row("ZF", *qs), row("ZI", *qs),
                cols[0:1, 16 + q:17 + q], Alu.add, Alu.max),
                r=[rkey("ZF"), rkey("ZI"), "mprev"], w=[rkey("MSC")], fragile=True)
            dve(lambda e, q=q, qs=qs: e.tensor_tensor_scan(
                row("FF", *qs), onesr[:, 0:128], row("ZF", *qs),
                0.0, Alu.mult, Alu.add),
                r=[rkey("ZF"), "onesr"], w=[rkey("FF")], fragile=True)
        dve(lambda e: e.tensor_tensor(row("WW"), row("FF"), row("MSC"), Alu.subtract),
            r=[rkey("FF"), rkey("MSC")], w=[rkey("WW")])
        dve(lambda e: e.tensor_tensor(row("UU"), row("ZI"), row("FF"), Alu.subtract),
            r=[rkey("ZI"), rkey("FF")], w=[rkey("UU")])
        for q in range(NSEQ):
            qs = (q * 128, (q + 1) * 128)
            dve(lambda e, q=q, qs=qs: e.tensor_scalar(
                row("ZI", *qs), row("WW", *qs), cols[0:1, 16 + q:17 + q],
                None, Alu.add),
                r=[rkey("WW"), "mprev"], w=[rkey("ZI")])
            dve(lambda e, q=q, qs=qs: e.tensor_scalar(
                row("ZF", *qs), row("UU", *qs), row("WW", qs[1] - 1, qs[1]),
                None, Alu.add),
                r=[rkey("UU"), rkey("WW")], w=[rkey("ZF")])
        for q in range(NSEQ):
            dve(lambda e, q=q: e.tensor_copy(
                cols[0:1, 16 + q:17 + q], row("MSC", (q + 1) * 128 - 1, (q + 1) * 128)),
                r=[rkey("MSC")], w=["mprev"])
        act(lambda e: e.activation(row("ALPHA"), row("ZI"), AF.Exp),
            r=[rkey("ZI")], w=[rkey("ALPHA")])
        act(lambda e: e.activation(row("BETA"), row("ZF"), AF.Exp),
            r=[rkey("ZF")], w=[rkey("BETA")])

        qlist = [int(c) for c in os.environ.get("KSEQ", "0123")]
        KAB = os.environ.get("KAB", "abcdefgijk")
        for q in qlist:
            qs0 = q * 128
            if "a" in KAB:
                for mc in range(MCN):
                    mm(p[2][:], nrm[:, mc * 512 + qs0:mc * 512 + qs0 + 128],
                       wv[:, mc * 512:(mc + 1) * 512],
                       r=[f"nrm{mc}", "wv"], w=["P2"], start=(mc == 0), stop=(mc == MCN - 1))
                act(lambda e: e.activation(vtok[:], p[2][:], AF.Copy),
                    r=["P2"], w=["vtok"])
                for mc in range(MCN):
                    mm(p[3][:], nrm[:, mc * 512 + qs0:mc * 512 + qs0 + 128],
                       wo[:, mc * 512:(mc + 1) * 512],
                       r=[f"nrm{mc}", "wo"], w=["P3"], start=(mc == 0), stop=(mc == MCN - 1))
                dve(lambda e: e.scalar_tensor_tensor(ztmp[:], p[3][:], 1.0, bo_bc[:],
                                                     Alu.mult, Alu.add),
                    r=["P3", "bo_bc"], w=["ztmp"])
                act(lambda e: e.activation(otok[:], ztmp[:], AF.Sigmoid),
                    r=["ztmp"], w=["otok"])
            else:
                dve(lambda e: e.memset(vtok[:], 0.01), r=["P2"], w=["vtok"])
                dve(lambda e: e.memset(otok[:], 0.5), r=["P3"], w=["otok"])
            if "b" in KAB:
                col_of(row("UU", qs0, qs0 + 128), p[2][:, 0:1], [rkey("UU")], "P2")
                dve(lambda e: e.tensor_copy(cols[:, CU:CU + 1], p[2][:, 0:1]),
                    r=["P2"], w=["colsu"])
                col_of(row("ALPHA", qs0, qs0 + 128), p[2][:, 1:2], [rkey("ALPHA")], "P2")
                dve(lambda e: e.tensor_copy(cols[:, CA:CA + 1], p[2][:, 1:2]),
                    r=["P2"], w=["colsa"])
                col_of(row("BETA", qs0, qs0 + 128), p[2][:, 2:3], [rkey("BETA")], "P2")
                dve(lambda e: e.tensor_copy(cols[:, CB:CB + 1], p[2][:, 2:3]),
                    r=["P2"], w=["colsb"])
                bcast(row("ALPHA", qs0 + 127, qs0 + 128), p[2][:, 3:4], [rkey("ALPHA")], "P2")
                dve(lambda e: e.tensor_copy(cols[:, CL:CL + 1], p[2][:, 3:4]),
                    r=["P2"], w=["colsl"])
            else:
                dve(lambda e: e.memset(cols[:, 0:8], 0.25),
                    r=["P2", "colsu", "colsa", "colsb", "colsl"],
                    w=["colsu", "colsa", "colsb", "colsl", "colsd", "colsd2", "colsr"])
            if "c" in KAB:
                bcast(row("WW", qs0, qs0 + 128), p[3][:, 0:128], [rkey("WW")], "P3")
                dve(lambda e: e.scalar_tensor_tensor(ztmp[:, 0:128], p[3][:, 0:128],
                                                     cols[:, CU:CU + 1], maskn[:],
                                                     Alu.add, Alu.add),
                    r=["P3", "colsu", "maskn"], w=["ztmp"])
                act(lambda e: e.activation(sbf[:, 640:768], ztmp[:, 0:128], AF.Exp),
                    r=["ztmp"], w=["dsb"])
            else:
                dve(lambda e: e.memset(sbf[:, 640:768], 0.125), r=["P3"], w=["dsb"])
            if "d" in KAB:
                for dc in range(DCN):
                    act(lambda e, dc=dc: e.activation(
                        sbf[:, dc * 128:(dc + 1) * 128],
                        qsb[:, dc * 512 + qs0:dc * 512 + qs0 + 128], AF.Copy),
                        r=["qsb"], w=["qbf"])
            else:
                dve(lambda e: e.memset(sbf[:, 0:512], 0.03), r=["qsb"], w=["qbf"])
            if "e" in KAB:
                for dc in range(DCN):
                    mm(p[2][:, 0:128], ksb[:, dc * 512 + qs0:dc * 512 + qs0 + 128],
                       sbf[:, dc * 128:(dc + 1) * 128],
                       r=["ksb", "qbf"], w=["P2"], start=(dc == 0), stop=(dc == DCN - 1))
                dve(lambda e: e.tensor_tensor(sbf[:, 512:640], sbf[:, 640:768],
                                              p[2][:, 0:128], Alu.mult),
                    r=["dsb", "P2"], w=["stl"])
            else:
                dve(lambda e: e.memset(sbf[:, 512:640], 0.02),
                    r=["dsb", "P2", "ksb", "qbf"], w=["stl"])
            if "g" in KAB:
                for jc in range(DCN):
                    mm(p[1][:], qsb[:, jc * 512 + qs0:jc * 512 + qs0 + 128],
                       CT[:, q * 2048 + jc * 512:q * 2048 + (jc + 1) * 512],
                       r=["qsb", "CT"], w=["P1"], start=(jc == 0), stop=(jc == DCN - 1))
                for jc in range(DCN):
                    mm(p[3][:, 131:132], qsb[:, jc * 512 + qs0:jc * 512 + qs0 + 128],
                       nst[:, q * 4 + jc:q * 4 + jc + 1],
                       r=["qsb", "nst"], w=["P3"], start=(jc == 0), stop=(jc == DCN - 1))
            else:
                dve(lambda e: e.memset(p[1][:], 0.02), r=["qsb", "CT", "P1"], w=["P1"])
                dve(lambda e: e.memset(p[3][:, 131:132], 0.1), r=["P3", "nst"], w=["P3"])
            if "i" in KAB:
                for mc in range(MCN):
                    mm(p[4][:], nrm[:, mc * 512 + qs0:mc * 512 + qs0 + 128],
                       wk[:, mc * 512:(mc + 1) * 512],
                       r=[f"nrm{mc}", "wk"], w=["P4"], start=(mc == 0), stop=(mc == MCN - 1))
                dve(lambda e: e.tensor_scalar(ktok[:], p[4][:], cols[:, CB:CB + 1],
                                              None, Alu.mult),
                    r=["P4", "colsb"], w=["ktok"])
            else:
                dve(lambda e: e.memset(ktok[:], 0.02),
                    r=["nrm", "wk", "P4", "colsb"], w=["ktok"])
            if "f" in KAB:
                mm(p[0][:], sbf[:, 512:640], vtok[:], r=["stl", "vtok"], w=["P0"])
                mm(p[3][:, 130:131], sbf[:, 512:640], onescb[:],
                   r=["stl", "onescb"], w=["P3"])
            else:
                dve(lambda e: e.memset(p[0][:], 0.02), r=["stl", "vtok", "P0"], w=["P0"])
                dve(lambda e: e.memset(p[3][:, 130:131], 0.7), r=["P3"], w=["P3"])
            if dbg_this and q == qlist[-1] and "2" in os.environ.get("KD", "1234"):
                dve(lambda e: e.tensor_copy(ztmp[:], p[0][:]), r=["P0"], w=["ztmp"])
                dma("act", ddbg_st[:, 512:1024], ztmp[:], r=["ztmp"], w=["dbgout"])
                dve(lambda e: e.tensor_copy(ztmp[:], p[1][:]), r=["P1"], w=["ztmp"])
                dma("act", ddbg_st[:, 1024:1536], ztmp[:], r=["ztmp"], w=["dbgout"])
                dve(lambda e: e.tensor_copy(ztmp[:, 0:8], p[3][:, 128:136]),
                    r=["P3"], w=["ztmp"])
                dma("act", ddbg_st[:, 128:136], ztmp[:, 0:8], r=["ztmp"], w=["dbgout"])
            for jc in (range(DCN) if "j" in KAB else ()):
                pp = 4 + (jc % 2)
                mm(p[pp][:], ktok[:, jc * 128:(jc + 1) * 128], vtok[:],
                   r=["ktok", "vtok"], w=[PK[pp]])
                if dbg_this and q == qlist[-1] and jc == 0 and "4" in os.environ.get("KD", "1234"):
                    dve(lambda e, pp=pp: e.tensor_copy(ztmp[:], p[pp][:]),
                        r=[PK[pp]], w=["ztmp"])
                    dma("act", ddbg_st[:, 2560:3072], ztmp[:], r=["ztmp"], w=["dbgout"])
                dve(lambda e, jc=jc, pp=pp, q=q: e.scalar_tensor_tensor(
                    CT[:, q * 2048 + jc * 512:q * 2048 + (jc + 1) * 512],
                    CT[:, q * 2048 + jc * 512:q * 2048 + (jc + 1) * 512],
                    cols[:, CL:CL + 1], p[pp][:], Alu.mult, Alu.add),
                    r=["CT", "colsl", PK[pp]], w=["CT"])
                mm(p[3][:, 132 + jc:133 + jc], ktok[:, jc * 128:(jc + 1) * 128],
                   onescb[:], r=["ktok", "onescb"], w=["P3"])
                dve(lambda e, jc=jc, q=q: e.scalar_tensor_tensor(
                    nst[:, q * 4 + jc:q * 4 + jc + 1],
                    nst[:, q * 4 + jc:q * 4 + jc + 1],
                    cols[:, CL:CL + 1], p[3][:, 132 + jc:133 + jc], Alu.mult, Alu.add),
                    r=["nst", "colsl", "P3"], w=["nst"])
            dve(lambda e: e.tensor_scalar(ztmp[:], p[1][:], cols[:, CA:CA + 1],
                                          None, Alu.mult),
                r=["P1", "colsa"], w=["ztmp"])
            dve(lambda e: e.tensor_tensor(ztmp[:], ztmp[:], p[0][:], Alu.add),
                r=["ztmp", "P0"], w=["ztmp"])
            dve(lambda e: e.tensor_scalar(cols[:, CD:CD + 1], p[3][:, 131:132],
                                          cols[:, CA:CA + 1], None, Alu.mult),
                r=["P3", "colsa"], w=["colsd"])
            dve(lambda e: e.tensor_tensor(cols[:, CD:CD + 1], cols[:, CD:CD + 1],
                                          p[3][:, 130:131], Alu.add),
                r=["colsd", "P3"], w=["colsd"])
            dve(lambda e: e.scalar_tensor_tensor(cols[:, CD2:CD2 + 1],
                                                 cols[:, CD:CD + 1], -1.0,
                                                 cols[:, CD:CD + 1], Alu.mult, Alu.max),
                r=["colsd"], w=["colsd2"])
            dve(lambda e: e.tensor_scalar(cols[:, CD2:CD2 + 1], cols[:, CD2:CD2 + 1],
                                          1.0, None, Alu.max),
                r=["colsd2"], w=["colsd2"])
            dve(lambda e: e.reciprocal(cols[:, CR:CR + 1], cols[:, CD2:CD2 + 1]),
                r=["colsd2"], w=["colsr"], fragile=True)
            dve(lambda e: e.tensor_tensor(ztmp[:], ztmp[:], otok[:], Alu.mult),
                r=["ztmp", "otok"], w=["ztmp"])
            dve(lambda e: e.tensor_scalar(htok[:], ztmp[:], cols[:, CR:CR + 1],
                                          None, Alu.mult),
                r=["ztmp", "colsr"], w=["htok"])
            for dc in (range(DCN) if "k" in KAB else ()):
                pp = dc % 2
                trans(p[pp][:, 0:128], htok[:, dc * 128:(dc + 1) * 128], identf[:],
                      r=["htok", "identf"], w=[PK[pp]])
                dve(lambda e, pp=pp, dc=dc: e.tensor_copy(
                    ztmp[:, dc * 128:(dc + 1) * 128], p[pp][:, 0:128]),
                    r=[PK[pp]], w=["ztmp"])
                dve(lambda e, dc=dc, q=q: e.tensor_copy(
                    ksb[:, dc * 512 + qs0:dc * 512 + qs0 + 128],
                    ztmp[:, dc * 128:(dc + 1) * 128]),
                    r=["ztmp"], w=["ksb"])

        for _d in range(8):
            mm(p[6][0:1, 0:1], onescb[:], onescb[:],
               r=["onescb"], w=["P6"])
        piece, pk = ring_fetch(dwout.rearrange("p a f -> p (a f)"), "dwout")
        for mc in range(MCN):
            pp = mc % 2
            for dc in range(DCN):
                mm(p[pp][:], piece[:, dc * 1024 + mc * 128:dc * 1024 + (mc + 1) * 128],
                   ksb[:, dc * 512:(dc + 1) * 512],
                   r=[pk, "ksb"], w=[PK[pp]], start=(dc == 0), stop=(dc == DCN - 1))
            dve(lambda e, mc=mc, pp=pp: e.tensor_tensor(
                x_sb[:, mc * 512:(mc + 1) * 512], x_sb[:, mc * 512:(mc + 1) * 512],
                p[pp][:], Alu.add),
                r=["x_sb", PK[pp]], w=["x_sb"])

    def dump_debug():
        def dump_f32(dst, src_ap, key):
            dma("act", dst, src_ap, r=[key], w=["dbgout"])
        def dump_bf(dst, src_ap, key, n):
            for c0 in range(0, n, 512):
                dve(lambda e, c0=c0, s=src_ap: e.tensor_copy(
                    ztmp[:, 0:512], s[:, c0:c0 + 512]), r=[key], w=["ztmp"])
                dma("act", dst[:, c0:c0 + 512], ztmp[:], r=["ztmp"], w=["dbgout"])
        dump_f32(ddbg_rb, rb[:], "R0")
        dump_f32(ddbg_cols, cols[:], "colsr")
        dump_f32(ddbg_qsb, qsb[:], "qsb")
        dump_f32(ddbg_ct, CT[:], "CT")
        dump_f32(ddbg_nst, nst[:], "nst")
        dump_bf(ddbg_ksb, ksb, "ksb", 2048)
        dump_bf(ddbg_htok, htok, "htok", 512)
        dump_bf(ddbg_vtok, vtok, "vtok", 512)
        dump_bf(ddbg_otok, otok, "otok", 512)
        dump_bf(ddbg_nrm, nrm, "nrm", 4096)
        dump_bf(ddbg_sbf, sbf, "qbf", 1024)

    load_x_group(0)
    for g in range(NG):
        ln_to_nrm()
        ffn_sweep()
        if stage >= 2:
            ln2_attn_in()
            attn_group(g, dbg_this=(dbg and g == 0))
            if dbg and g == 0:
                dump_debug()
        if stage >= 3:
            ln_to_nrm()
            ffn_sweep()
        store_x_group(g)
        if g + 1 < NG:
            load_x_group(g + 1)

    A.resolve()
    fin = A.final

    with es:
        sem_cm = ExitStack()
        sems = {}
        for nm in ("pe", "dve", "act", "sp", "pool"):
            sems[nm] = sem_cm.enter_context(nc.semaphore(f"s_{nm}"))
        for qq in ("sp", "act", "pool"):
            for ln in range(Asm.NLANE):
                sems[f"{qq}_dma{ln}"] = sem_cm.enter_context(
                    nc.semaphore(f"s_{qq}d{ln}"))
        with sem_cm, nc.Block() as blk:

            @blk.tensor
            def _(e):
                A.play("pe", e, sems)

            @blk.vector
            def _(e):
                A.play("dve", e, sems)

            @blk.scalar
            def _(e):
                A.play("act", e, sems)
                for ln in range(Asm.NLANE):
                    k = f"act_dma{ln}"
                    if k in fin:
                        e.wait_ge(sems[k], fin[k])

            @blk.sync
            def _(e):
                A.play("sp", e, sems)
                for ln in range(Asm.NLANE):
                    k = f"sp_dma{ln}"
                    if k in fin:
                        e.wait_ge(sems[k], fin[k])

            @blk.gpsimd
            def _(e):
                A.play("pool", e, sems)
                for ln in range(Asm.NLANE):
                    k = f"pool_dma{ln}"
                    if k in fin:
                        e.wait_ge(sems[k], fin[k])

    return nc


def make_in_maps(L0_attn_x, L0_attn_norm_w, L0_attn_Wq, L0_attn_Wk, L0_attn_Wv,
                 L0_attn_wi_w, L0_attn_wi_b, L0_attn_wf_w, L0_attn_wf_b,
                 L0_attn_Wo, L0_attn_bo, L0_attn_Wout,
                 L0_ffn_k0, L0_ffn_k1, L0_ffn_k2, L0_ffn_s0, L0_ffn_s1, L0_ffn_s2):
    x = np.asarray(L0_attn_x, dtype=np.float32)
    asf = lambda a: np.ascontiguousarray(np.asarray(a, dtype=np.float32))
    wqT = asf(np.asarray(L0_attn_Wq).T)
    wkT = asf(np.asarray(L0_attn_Wk).T)
    wvT = asf(np.asarray(L0_attn_Wv).T)
    woT = asf(np.asarray(L0_attn_Wo).T)
    woutT = asf(np.asarray(L0_attn_Wout).T)
    nwr = asf(np.asarray(L0_attn_norm_w).reshape(8, 128).T)
    wir = asf(np.asarray(L0_attn_wi_w).reshape(8, 128).T)
    wfr = asf(np.asarray(L0_attn_wf_w).reshape(8, 128).T)
    bor = asf(np.asarray(L0_attn_bo).reshape(1, DH))
    scal = np.zeros((1, 8), np.float32)
    scal[0, 0] = np.float32(np.asarray(L0_ffn_s0))
    scal[0, 1] = np.float32(np.asarray(L0_ffn_s1))
    scal[0, 2] = np.float32(np.asarray(L0_ffn_s2))
    scal[0, 3] = np.float32(np.asarray(L0_attn_wi_b).reshape(-1)[0])
    scal[0, 4] = np.float32(np.asarray(L0_attn_wf_b).reshape(-1)[0])
    k0u = np.ascontiguousarray(np.asarray(L0_ffn_k0)).view(np.uint8)
    k1u = np.ascontiguousarray(np.asarray(L0_ffn_k1)).view(np.uint8)
    k2u = np.ascontiguousarray(np.asarray(L0_ffn_k2)).view(np.uint8)
    idb = np.eye(128, dtype=BFD)
    idf = np.eye(128, dtype=np.float32)
    mskn = np.where(np.arange(128)[:, None] <= np.arange(128)[None, :],
                    np.float32(0.0), np.float32(NEGBIG)).astype(np.float32)
    in_maps = []
    for c in range(NCORES):
        in_maps.append({
            "x": np.ascontiguousarray(x[c * NSEQ:(c + 1) * NSEQ]),
            "wqT": wqT, "wkT": wkT, "wvT": wvT, "woT": woT, "woutT": woutT,
            "nwr": nwr, "wir": wir, "wfr": wfr, "bor": bor, "scal": scal,
            "k0u": k0u, "k1u": k1u, "k2u": k2u,
            "idb": idb, "idf": idf, "mskn": mskn,
        })
    return in_maps


def kernel(**inputs):
    in_maps = make_in_maps(**inputs)
    nc = build_program()
    res = run_bass_kernel_spmd(nc, in_maps, core_ids=list(range(NCORES)))
    out = np.concatenate([np.asarray(r["out"]) for r in res.results], axis=0)
    return out.astype(np.float32)
